# version 24
# baseline (speedup 1.0000x reference)
import math
from contextlib import ExitStack

import numpy as np
import ml_dtypes

import concourse.bass as bass
import concourse.mybir as mybir
from concourse.bass_utils import run_bass_kernel_spmd

F32 = mybir.dt.float32
BF16 = mybir.dt.bfloat16
I8 = mybir.dt.int8
U8 = mybir.dt.uint8
AF = mybir.ActivationFunctionType
ALU = mybir.AluOpType
AX = mybir.AxisListType

N_CORES = 8
DMA_ENGS = ("sync", "gpsimd")
ALL_ENGS = ("tensor", "scalar", "vector", "gpsimd", "sync")


class _Op:
    __slots__ = ("eng", "fn", "deps", "inc", "idx", "seq", "tag", "rd", "wr",
                 "chan")

    def __init__(self, eng, fn, idx, tag):
        self.eng = eng
        self.fn = fn
        self.deps = []
        self.inc = False
        self.idx = idx
        self.seq = None
        self.tag = tag
        self.chan = None


class Plan:
    def __init__(self):
        self.ops = []
        self.res = {}

    def op(self, eng, fn, reads=(), writes=(), tag="", chan=None):
        o = _Op(eng, fn, len(self.ops), tag)
        if chan is not None:
            assert eng in DMA_ENGS
            o.chan = ("dma", chan, eng)
        dep = {}

        def add(op2):
            if op2 is None:
                return
            if op2.eng == eng and eng not in DMA_ENGS:
                return
            if dep.get(op2.eng) is None or dep[op2.eng].idx < op2.idx:
                dep[op2.eng] = op2

        for k in reads:
            st = self.res.get(k)
            if st:
                for op2 in st["w"].values():
                    add(op2)
        for k in writes:
            st = self.res.get(k)
            if st:
                for op2 in st["w"].values():
                    add(op2)
                for op2 in st["r"].values():
                    add(op2)
        o.deps = list(dep.values())
        o.rd = tuple(reads)
        o.wr = tuple(writes)
        for d in o.deps:
            d.inc = True
        if o.chan is not None:
            o.inc = True
        for k in reads:
            self.res.setdefault(k, {"w": {}, "r": {}})["r"][eng] = o
        for k in writes:
            self.res[k] = {"w": {eng: o}, "r": {}}
        self.ops.append(o)
        return o

    def fence(self, old_keys, new_keys):
        mw, mr = {}, {}
        for k in old_keys:
            st = self.res.get(k)
            if not st:
                continue
            for e, o2 in st["w"].items():
                if e not in mw or mw[e].idx < o2.idx:
                    mw[e] = o2
            for e, o2 in st["r"].items():
                if e not in mr or mr[e].idx < o2.idx:
                    mr[e] = o2
        for k in new_keys:
            st = self.res.setdefault(k, {"w": {}, "r": {}})
            for e, o2 in mw.items():
                if e not in st["w"] or st["w"][e].idx < o2.idx:
                    st["w"][e] = o2
            for e, o2 in mr.items():
                if e not in st["r"] or st["r"][e].idx < o2.idx:
                    st["r"][e] = o2

    def finalize(self):
        cum = {}
        for o in self.ops:
            key = o.chan if o.chan is not None else o.eng
            if o.inc:
                cum[key] = cum.get(key, 0) + (16 if o.chan is not None else 1)
                o.seq = cum[key]
        self.cum = cum
        self.per_eng = {e: [] for e in ALL_ENGS}
        for o in self.ops:
            self.per_eng[o.eng].append(o)
        self.channels = sorted({o.chan for o in self.ops if o.chan is not None})
        self.chan_eng = {}
        for o in self.ops:
            if o.chan is not None:
                self.chan_eng[o.chan] = o.eng

    def emit_engine(self, ename, eng, sems, final_chans=()):
        waited = {}
        for o in self.per_eng[ename]:
            need = {}
            for d in o.deps:
                key = d.chan if d.chan is not None else d.eng
                if need.get(key, 0) < d.seq:
                    need[key] = d.seq
            for s, v in sorted(need.items(), key=lambda kv: str(kv[0])):
                if waited.get(s, 0) < v:
                    eng.wait_ge(sems[s], v)
                    waited[s] = v
            instr = o.fn(eng)
            if o.inc:
                key = o.chan if o.chan is not None else o.eng
                instr.then_inc(sems[key], 16 if o.chan is not None else 1)
        if ename in DMA_ENGS:
            for ch, e2 in sorted(self.chan_eng.items(), key=lambda kv: str(kv[0])):
                if e2 == ename and waited.get(ch, 0) < self.cum[ch]:
                    eng.wait_ge(sems[ch], self.cum[ch])


def build_program(S, D, I, H, debug_scratch=False):
    KC = D // 128
    ICC = I // 128
    SC = S // 128
    J = 512
    TS = J // 128
    NJ = S // J
    OGD = D // 512
    OGP = max(1, min(OGD, 8 // TS))
    SCALE = 1.0 / math.sqrt(128.0)
    EPS = 1e-5
    assert KC == H and D % 512 == 0 and S % J == 0 and I % 256 == 0

    nc = bass.Bass("TRN2", target_bir_lowering=False)

    x_h = nc.dram_tensor("x", [S, D], F32, kind="ExternalInput").ap()
    qT_h = nc.dram_tensor("qT", [D, D], F32, kind="ExternalInput").ap()
    kT_h = nc.dram_tensor("kT", [D, D], F32, kind="ExternalInput").ap()
    vT_h = nc.dram_tensor("vT", [D, D], F32, kind="ExternalInput").ap()
    oT_h = nc.dram_tensor("oT", [D, D], F32, kind="ExternalInput").ap()
    wg_h = nc.dram_tensor("wg", [D, I], I8, kind="ExternalInput").ap()
    wu_h = nc.dram_tensor("wu", [D, I], I8, kind="ExternalInput").ap()
    wd_h = nc.dram_tensor("wd", [I, D], I8, kind="ExternalInput").ap()
    qb_h = nc.dram_tensor("qb", [128, KC], F32, kind="ExternalInput").ap()
    kb_h = nc.dram_tensor("kb", [128, KC], F32, kind="ExternalInput").ap()
    vb_h = nc.dram_tensor("vb", [128, KC], F32, kind="ExternalInput").ap()
    ob_h = nc.dram_tensor("ob", [128, KC], F32, kind="ExternalInput").ap()
    sg_h = nc.dram_tensor("sg", [128, ICC], F32, kind="ExternalInput").ap()
    su_h = nc.dram_tensor("su", [128, ICC], F32, kind="ExternalInput").ap()
    sd_h = nc.dram_tensor("sd", [128, KC], F32, kind="ExternalInput").ap()
    sdf_h = nc.dram_tensor("sdf", [1, D], F32, kind="ExternalInput").ap()
    obf_h = nc.dram_tensor("obf", [1, D], F32, kind="ExternalInput").ap()
    cos_h = nc.dram_tensor("cosT", [128, S], BF16, kind="ExternalInput").ap()
    sin_h = nc.dram_tensor("sinT", [128, S], BF16, kind="ExternalInput").ap()
    idb_h = nc.dram_tensor("identb", [128, 128], BF16, kind="ExternalInput").ap()
    id32_h = nc.dram_tensor("ident32", [128, 128], F32, kind="ExternalInput").ap()
    rot_h = nc.dram_tensor("rot", [128, 128], BF16, kind="ExternalInput").ap()
    onesb_h = nc.dram_tensor("onesb", [128, 128], BF16, kind="ExternalInput").ap()
    y_h = nc.dram_tensor("y", [S, D], F32, kind="ExternalOutput").ap()

    wgb = nc.dram_tensor("wgb", [I // 256, 128, KC * 256], BF16, kind="Internal").ap()
    wub = nc.dram_tensor("wub", [I // 256, 128, KC * 256], BF16, kind="Internal").ap()
    qTb = nc.dram_tensor("qTb", [D // 256, 128, KC * 256], BF16, kind="Internal").ap()
    kTb = nc.dram_tensor("kTb", [D // 256, 128, KC * 256], BF16, kind="Internal").ap()
    vTb = nc.dram_tensor("vTb", [D // 256, 128, KC * 256], BF16, kind="Internal").ap()
    oTb = nc.dram_tensor("oTb", [D // 256, 128, KC * 256], BF16, kind="Internal").ap()
    skind = "ExternalOutput" if debug_scratch else "Internal"
    Kscr = nc.dram_tensor("Kscr", [D, S], BF16, kind=skind).ap()
    Qscr = nc.dram_tensor("Qscr", [D, S], BF16, kind=skind).ap()
    VTscr = nc.dram_tensor("VTscr", [H, 128, SC, 128], BF16, kind=skind).ap()
    x1scr = nc.dram_tensor("x1scr", [S, D], F32, kind=skind).ap()
    dbgO = (nc.dram_tensor("dbgO", [128, H * 512], BF16, kind="ExternalOutput").ap()
            if debug_scratch else None)
    dbgXLN = (nc.dram_tensor("dbgXLN", [128, (D // 128) * 512], BF16,
                             kind="ExternalOutput").ap() if debug_scratch else None)
    dbgST = (nc.dram_tensor("dbgST", [128, 24], F32,
                            kind="ExternalOutput").ap() if debug_scratch else None)
    dbgFU = (nc.dram_tensor("dbgFU", [128, (I // 128) * 512], BF16,
                            kind="ExternalOutput").ap() if debug_scratch else None)
    dbgX2 = (nc.dram_tensor("dbgX2", [S // 128, 128, D], F32,
                            kind="ExternalOutput").ap() if debug_scratch else None)

    ctx = ExitStack()
    sb = lambda n, s, d: ctx.enter_context(nc.sbuf_tensor(n, s, d))

    PT_BYTES = SC * J * 2
    AR_BYTES = max(2 * PT_BYTES, 2 * ICC * J, 8 * D + 2 * TS * D, 40960)
    x_tok = sb("x_tok", [128, TS, D], F32)
    xln_fm = sb("xln_fm", [128, KC, J], BF16)
    arena = sb("arena", [128, AR_BYTES], U8)
    arena_ap = arena.ap()
    sq_v = arena_ap[:, 0:8 * D].bitcast(F32).rearrange("p (a b) -> p a b", b=D)
    xln_tok = arena_ap[:, 8 * D:8 * D + 2 * TS * D].bitcast(BF16).rearrange(
        "p (t d) -> p t d", d=D)
    fuse_v = arena_ap[:, 0:2 * ICC * J].bitcast(BF16).rearrange(
        "p (c j) -> p c j", j=J)
    PT = [arena_ap[:, i * PT_BYTES:(i + 1) * PT_BYTES].bitcast(BF16)
          for i in (0, 1)]
    KEY_LN = ["sq0", "sq1"] + [f"xlt{t}" for t in range(TS)]
    KEY_FU = [f"fu{i}" for i in range(ICC)]
    KEY_PT = [f"PT{p}_{pr}" for p in (0, 1) for pr in range((SC + 1) // 2)]
    KEY_P0 = [f"p0s{s}" for s in range(4)]
    ALL_AR = KEY_LN + KEY_FU + KEY_PT + KEY_P0

    wbufA = [sb(f"wbufA{i}", [128, KC, 256], BF16) for i in (0, 1)]
    wbufB = [sb(f"wbufB{i}", [128, KC, 256], BF16) for i in (0, 1)]
    wdbuf2 = [wbufA[i].ap().rearrange("p a b -> p (a b)")[:, 0:2 * OGP * 512]
              .rearrange("p (c d) -> p c d", d=OGP * 512) for i in (0, 1)]
    gate_s = sb("gate_s", [128, J], F32)
    SHB = max(4 * 6144, 4 * S + 4 * SC * 128)
    shbuf = sb("shbuf", [128, SHB], U8)
    shb_ap = shbuf.ap()
    Kbuf = [shb_ap[:, i * 2 * S:(i + 1) * 2 * S].bitcast(BF16) for i in (0, 1)]
    Vbuf = [shb_ap[:, 4 * S + i * 2 * SC * 128:
                   4 * S + (i + 1) * 2 * SC * 128].bitcast(BF16).rearrange(
        "p (a b) -> p a b", b=128) for i in (0, 1)]
    Qbuf = [sb(f"Qbuf{i}", [128, J], BF16) for i in (0, 1)]
    O_fm = sb("O_fm", [128, H, J], BF16)
    wstgD = [sb(f"wstgD{i}", [128, 2 * OGP * 512], I8).ap() for i in (0, 1)]
    rt1 = sb("rt1", [128, J], F32)
    rden = rt1
    vt_sb = gate_s.ap()[:, 0:TS * 64].bitcast(BF16).rearrange(
        "p (t d) -> p t d", d=128)
    stats = sb("stats", [128, 6 * TS], F32)
    Chb = sb("Chb", [128, H], F32)
    m128 = sb("m128", [128, 2], F32)
    m128b = sb("m128b", [128, 1], BF16)
    crow = sb("crow", [1, 2], F32)
    cosJ = sb("cosJ", [128, J], BF16)
    sinJ = sb("sinJ", [128, J], BF16)
    bcbuf = sb("bcbuf", [128, 512], F32)
    q_sb = [bcbuf.ap()[:, 0:256].bitcast(BF16),
            bcbuf.ap()[:, 256:512].bitcast(BF16)]
    identb = sb("identb_s", [128, 128], BF16)
    rotm = sb("rot_s", [128, 128], BF16)
    onesb = sb("onesb_s", [128, 128], BF16)
    crowb = sb("crowb", [1, 2], BF16)
    qb_s = sb("qb_s", [128, KC], F32)
    kb_s = sb("kb_s", [128, KC], F32)
    vb_s = sb("vb_s", [128, KC], F32)
    sg_s = sb("sg_s", [128, ICC], F32)
    su_s = sb("su_s", [128, ICC], F32)

    pbig = [ctx.enter_context(nc.psum_tensor(f"pp{i}", [128, 1024], F32))
            for i in (0, 1)]
    psin = [ctx.enter_context(nc.psum_tensor(f"pb{i}", [128, 512], F32))
            for i in (4, 5, 6, 7)]

    def bank(i):
        if i < 4:
            return pbig[i // 2].ap()[:, (i % 2) * 512:(i % 2 + 1) * 512]
        return psin[i - 4].ap()

    def bank_bf(i):
        if i < 4:
            return pbig[i // 2].ap().bitcast(BF16)[:, (i % 2) * 1024:(i % 2 + 1) * 1024]
        return psin[i - 4].ap().bitcast(BF16)

    def pair(p):
        return pbig[p].ap()

    sems = {e: ctx.enter_context(nc.semaphore(f"sem_{e}"))
            for e in ("tensor", "scalar", "vector", "gpsimd")}
    P = Plan()

    def dma(queue, out, in_, reads=(), writes=(), tag="", chan=None):
        assert chan is not None
        P.op(queue, lambda e, o=out, i=in_: e.dma_start(out=o, in_=i),
             reads=reads, writes=writes, tag=tag, chan=chan)

    def mm(lhsT, rhs, out, start, stop, reads, writes, tag=""):
        P.op("tensor",
             lambda e, a=lhsT, b=rhs, o=out, s1=start, s2=stop:
             e.matmul(o, a, b, start=s1, stop=s2),
             reads=reads, writes=writes, tag=tag)

    def transp(out, in_, ident, reads, writes, tag=""):
        P.op("tensor",
             lambda e, o=out, i=in_, d=ident: e.transpose(o, i, d),
             reads=reads, writes=writes, tag=tag)

    for dst, src in [(identb, idb_h), (rotm, rot_h),
                     (onesb, onesb_h), (qb_s, qb_h),
                     (kb_s, kb_h), (vb_s, vb_h), (sg_s, sg_h), (su_s, su_h)]:
        dma("sync", dst[:], src, reads=(), writes=(f"c:{dst.name}",),
            tag="p0c", chan=f"c_{dst.name}")

    NQ_WG = (I + 2047) // 2048
    NH_Q = (D + 1023) // 1024
    cv_state = {"sl": [0, 0], "wg": 0, "wd": 0}

    def cv_chunk(src_ap, dst_ap, kc, c0, cw, src_dt, par, key, tag, og_tiled,
                 cast=True):
        sli = cv_state["sl"][par]
        cv_state["sl"][par] ^= 1
        sl = par * 2 + sli
        esz = 4 if src_dt == F32 else 1
        base = sl * 6144
        vin = shb_ap[:, base:base + cw * esz].bitcast(src_dt)
        vout = shb_ap[:, base + cw * esz:base + cw * esz + 2 * cw].bitcast(BF16)
        skey = f"p0s{sl}"
        rows = slice(kc * 128, (kc + 1) * 128)
        dma("sync", vin[:, 0:cw], src_ap[rows, c0:c0 + cw],
            reads=(), writes=(skey,), tag=tag, chan=f"p0in{par}")
        if cast:
            if par == 0:
                P.op("vector", lambda e, o=vout, i=vin, w=cw:
                     e.tensor_copy(o[:, 0:w], i[:, 0:w]),
                     reads=(), writes=(skey,), tag=tag)
            else:
                P.op("scalar", lambda e, o=vout, i=vin, w=cw:
                     e.copy(o[:, 0:w], i[:, 0:w]),
                     reads=(), writes=(skey,), tag=tag)
        vsrc = vout if cast else vin
        if og_tiled:
            out_ap = dst_ap[c0 // 256:(c0 + cw) // 256, :,
                            kc * 256:(kc + 1) * 256].rearrange("og p o -> p og o")
            in_ap = vsrc[:, 0:cw].rearrange("p (og o) -> p og o", o=256)
        else:
            out_ap = dst_ap[rows, c0:c0 + cw]
            in_ap = vsrc[:, 0:cw]
        dma("gpsimd" if par == 0 else "sync", out_ap, in_ap, reads=(skey,),
            writes=(key,), tag=tag, chan=f"scr_{dst_ap.tensor.name}")

    def cv_wgwu(q):
        cw = min(I, 2048)
        for kc in range(KC):
            cv_chunk(wg_h, wgb, kc, q * 2048, cw, I8, 0, f"scr:wgb:q{q}",
                     f"cvg{q}", True)
        for kc in range(KC):
            cv_chunk(wu_h, wub, kc, q * 2048, cw, I8, 1, f"scr:wub:q{q}",
                     f"cvu{q}", True)

    def cv_wgwu_next():
        if cv_state["wg"] < NQ_WG:
            cv_wgwu(cv_state["wg"])
            cv_state["wg"] += 1

    qkvo_groups = [(s, d, hh) for (s, d) in
                   ((qT_h, qTb), (kT_h, kTb), (vT_h, vTb), (oT_h, oTb))
                   for hh in range(NH_Q)]

    def cv_qkvo_next():
        if not qkvo_groups:
            return
        s, d, hh = qkvo_groups.pop(0)
        cw = min(D, 1024)
        for kc in range(KC):
            cv_chunk(s, d, kc, hh * 1024, cw, F32, kc % 2,
                     f"scr:{d.tensor.name}:h{hh}", f"cvq{d.tensor.name}{hh}",
                     True)

    def load_bc(kind, og, tag):
        P.fence(["qsb0", "qsb1"], ["c:bc"])
        srcb = sdf_h if kind == "sd" else obf_h
        dma("sync", bcbuf[:, :],
            srcb[0:1, og * 512:(og + 1) * 512].broadcast_to([128, 512]),
            reads=(), writes=("c:bc",), tag=tag, chan="bcbuf")
        if kind == "sd":
            P.op("scalar", lambda e: e.mul(bcbuf[:, :], bcbuf[:, :], 0.5),
                 reads=(), writes=("c:bc",), tag=tag)

    spart = sb("spart", [128, 2 * TS * OGD], F32)

    def inc_stats(og, ts, b, tag):
        P.op("vector", lambda e, og=og, ts=ts:
             e.reduce_sum(spart[:, ts * OGD + og:ts * OGD + og + 1],
                          x_tok[:, ts, og * 512:(og + 1) * 512], axis=AX.X),
             reads=(f"x{ts}",), writes=("spart_s",), tag=tag)
        P.op("vector", lambda e, og=og, ts=ts, b=b:
             e.scalar_tensor_tensor(
                 bank(b), x_tok[:, ts, og * 512:(og + 1) * 512], 1.0,
                 x_tok[:, ts, og * 512:(og + 1) * 512],
                 op0=ALU.mult, op1=ALU.mult,
                 accum_out=spart[:, TS * OGD + ts * OGD + og:
                                 TS * OGD + ts * OGD + og + 1]),
             reads=(f"x{ts}",), writes=(f"ps{b}", "spart_q"), tag=tag)

    def ln_stats_finish(tag):
        for ts in range(TS):
            P.op("vector", lambda e, ts=ts:
                 e.reduce_sum(stats[:, ts:ts + 1],
                              spart[:, ts * OGD:(ts + 1) * OGD], axis=AX.X),
                 reads=("spart_s",), writes=("st_sum",), tag=tag)
            P.op("vector", lambda e, ts=ts:
                 e.reduce_sum(stats[:, TS + ts:TS + ts + 1],
                              spart[:, TS * OGD + ts * OGD:
                                    TS * OGD + (ts + 1) * OGD], axis=AX.X),
                 reads=("spart_q",), writes=("st_sq",), tag=tag)
        ln_stats_tail(tag)

    def ln_stats(tag):
        P.fence(KEY_FU + KEY_PT, KEY_LN)
        for ts in range(TS):
            sl = ts % 2
            P.op("scalar", lambda e, ts=ts, sl=sl:
                 e.activation(sq_v[:, sl, :], x_tok[:, ts, :], AF.Square),
                 reads=(f"x{ts}",), writes=(f"sq{sl}",), tag=tag)
            P.op("vector", lambda e, ts=ts:
                 e.reduce_sum(stats[:, ts:ts + 1], x_tok[:, ts, :], axis=AX.X),
                 reads=(f"x{ts}",), writes=("st_sum",), tag=tag)
            P.op("vector", lambda e, ts=ts, sl=sl:
                 e.reduce_sum(stats[:, TS + ts:TS + ts + 1], sq_v[:, sl, :],
                              axis=AX.X),
                 reads=(f"sq{sl}",), writes=("st_sq",), tag=tag)
        ln_stats_tail(tag)

    def ln_stats_tail(tag):
        P.op("vector", lambda e:
             e.tensor_scalar_mul(stats[:, 2 * TS:3 * TS], stats[:, 0:TS],
                                 -1.0 / D),
             reads=("st_sum",), writes=("st_nm",), tag=tag)
        P.op("vector", lambda e:
             e.tensor_scalar_mul(stats[:, 4 * TS:5 * TS], stats[:, TS:2 * TS],
                                 1.0 / D),
             reads=("st_sq",), writes=("st_e2",), tag=tag)
        P.op("vector", lambda e:
             e.tensor_mul(stats[:, 5 * TS:6 * TS], stats[:, 2 * TS:3 * TS],
                          stats[:, 2 * TS:3 * TS]),
             reads=("st_nm",), writes=("st_m2",), tag=tag)
        P.op("vector", lambda e:
             e.tensor_sub(stats[:, 4 * TS:5 * TS], stats[:, 4 * TS:5 * TS],
                          stats[:, 5 * TS:6 * TS]),
             reads=("st_m2",), writes=("st_e2",), tag=tag)
        P.op("vector", lambda e:
             e.tensor_scalar_add(stats[:, 4 * TS:5 * TS],
                                 stats[:, 4 * TS:5 * TS], EPS),
             reads=(), writes=("st_e2",), tag=tag)
        P.op("scalar", lambda e:
             e.activation(stats[:, 5 * TS:6 * TS], stats[:, 4 * TS:5 * TS],
                          AF.Ln),
             reads=("st_e2",), writes=("st_m2",), tag=tag)
        P.op("scalar", lambda e:
             e.activation(stats[:, 3 * TS:4 * TS], stats[:, 5 * TS:6 * TS],
                          AF.Exp, scale=-0.5),
             reads=("st_m2",), writes=("st_rs",), tag=tag)

    def ln_apply_tt(tag, early=False):
        P.fence(KEY_FU + KEY_PT, KEY_LN)
        for ts in range(TS):
            eng = "vector" if (ts % 2 == 0 or early) else "gpsimd"
            P.op(eng, lambda e, ts=ts:
                 e.tensor_scalar(xln_tok[:, ts, :], x_tok[:, ts, :],
                                 stats[:, 2 * TS + ts:2 * TS + ts + 1],
                                 stats[:, 3 * TS + ts:3 * TS + ts + 1],
                                 op0=ALU.add, op1=ALU.mult),
                 reads=(f"x{ts}", "st_nm", "st_rs"), writes=(f"xlt{ts}",),
                 tag=tag)

    def ln_apply_tr(tag):
        for cpair in range(KC // 2):
            b = 6 + (cpair % 2)
            for i in range(2):
                c = cpair * 2 + i
                for ts in range(TS):
                    transp(bank_bf(b)[:, (i * TS + ts) * 128:(i * TS + ts + 1) * 128],
                           xln_tok[:, ts, c * 128:(c + 1) * 128], identb[:],
                           reads=(f"xlt{ts}", "c:identb_s"), writes=(f"ps{b}",),
                           tag=tag)
            dst = xln_fm[:, cpair * 2:cpair * 2 + 2, :].rearrange(
                "p a b -> p (a b)")
            if cpair % 2 == 0:
                P.op("scalar", lambda e, d=dst, b=b:
                     e.copy(d, bank_bf(b)[:, 0:1024]),
                     reads=(), writes=(f"ps{b}", f"xlf{cpair * 2}",
                                       f"xlf{cpair * 2 + 1}"), tag=tag)
            else:
                P.op("vector", lambda e, d=dst, b=b:
                     e.tensor_copy(d, bank_bf(b)[:, 0:1024]),
                     reads=(), writes=(f"ps{b}", f"xlf{cpair * 2}",
                                       f"xlf{cpair * 2 + 1}"), tag=tag)

    def ln_apply(tag):
        ln_apply_tt(tag)
        ln_apply_tr(tag)

    def ln_finish_ts(ts, tag):
        P.op("vector", lambda e, ts=ts:
             e.reduce_sum(stats[:, ts:ts + 1],
                          spart[:, ts * OGD:(ts + 1) * OGD], axis=AX.X),
             reads=("spart_s",), writes=("st_sum",), tag=tag)
        P.op("vector", lambda e, ts=ts:
             e.reduce_sum(stats[:, TS + ts:TS + ts + 1],
                          spart[:, TS * OGD + ts * OGD:
                                TS * OGD + (ts + 1) * OGD], axis=AX.X),
             reads=("spart_q",), writes=("st_sq",), tag=tag)
        P.op("vector", lambda e, ts=ts:
             e.tensor_scalar_mul(stats[:, 2 * TS + ts:2 * TS + ts + 1],
                                 stats[:, ts:ts + 1], -1.0 / D),
             reads=("st_sum",), writes=("st_nm",), tag=tag)
        P.op("vector", lambda e, ts=ts:
             e.tensor_scalar_mul(stats[:, 4 * TS + ts:4 * TS + ts + 1],
                                 stats[:, TS + ts:TS + ts + 1], 1.0 / D),
             reads=("st_sq",), writes=("st_e2",), tag=tag)
        P.op("vector", lambda e, ts=ts:
             e.tensor_mul(stats[:, 5 * TS + ts:5 * TS + ts + 1],
                          stats[:, 2 * TS + ts:2 * TS + ts + 1],
                          stats[:, 2 * TS + ts:2 * TS + ts + 1]),
             reads=("st_nm",), writes=("st_m2",), tag=tag)
        P.op("vector", lambda e, ts=ts:
             e.tensor_sub(stats[:, 4 * TS + ts:4 * TS + ts + 1],
                          stats[:, 4 * TS + ts:4 * TS + ts + 1],
                          stats[:, 5 * TS + ts:5 * TS + ts + 1]),
             reads=("st_m2",), writes=("st_e2",), tag=tag)
        P.op("vector", lambda e, ts=ts:
             e.tensor_scalar_add(stats[:, 4 * TS + ts:4 * TS + ts + 1],
                                 stats[:, 4 * TS + ts:4 * TS + ts + 1], EPS),
             reads=(), writes=("st_e2",), tag=tag)
        P.op("scalar", lambda e, ts=ts:
             e.activation(stats[:, 5 * TS + ts:5 * TS + ts + 1],
                          stats[:, 4 * TS + ts:4 * TS + ts + 1], AF.Ln),
             reads=("st_e2",), writes=("st_m2",), tag=tag)
        P.op("scalar", lambda e, ts=ts:
             e.activation(stats[:, 3 * TS + ts:3 * TS + ts + 1],
                          stats[:, 5 * TS + ts:5 * TS + ts + 1], AF.Exp,
                          scale=-0.5),
             reads=("st_m2",), writes=("st_rs",), tag=tag)
        P.fence(KEY_FU + KEY_PT, [f"xlt{ts}"])
        eng = "vector" if ts % 2 == 0 else "gpsimd"
        P.op(eng, lambda e, ts=ts:
             e.tensor_scalar(xln_tok[:, ts, :], x_tok[:, ts, :],
                             stats[:, 2 * TS + ts:2 * TS + ts + 1],
                             stats[:, 3 * TS + ts:3 * TS + ts + 1],
                             op0=ALU.add, op1=ALU.mult),
             reads=(f"x{ts}", "st_nm", "st_rs"), writes=(f"xlt{ts}",),
             tag=tag)

    def ffn(tag, collect_stats=False, og_hook=None, ic_hook=None):
        P.fence(KEY_LN + KEY_PT, KEY_FU)
        for og in range(ICC // 2):
            if og_hook is not None:
                og_hook(og)
            par = og % 2
            gsrc = wgb[og].rearrange("p (kc o) -> p kc o", o=256)
            usrc = wub[og].rearrange("p (kc o) -> p kc o", o=256)
            dma("sync", wbufA[par][:], gsrc, reads=(f"scr:wgb:q{og // 8}",),
                writes=(f"wA{par}",), tag=tag, chan=f"wA{par}")
            dma("gpsimd", wbufB[par][:], usrc, reads=(f"scr:wub:q{og // 8}",),
                writes=(f"wB{par}",), tag=tag, chan=f"wB{par}")
            for i in range(2):
                oc = og * 2 + i
                for kc in range(KC):
                    mm(wbufA[par][:, kc, i * 128:(i + 1) * 128],
                       xln_fm[:, kc, :], bank(i), kc == 0, kc == KC - 1,
                       reads=(f"wA{par}", f"xlf{kc}"), writes=(f"ps{i}",),
                       tag=tag)
                for kc in range(KC):
                    mm(wbufB[par][:, kc, i * 128:(i + 1) * 128],
                       xln_fm[:, kc, :], bank(2 + i), kc == 0, kc == KC - 1,
                       reads=(f"wB{par}", f"xlf{kc}"), writes=(f"ps{2 + i}",),
                       tag=tag)
                P.op("scalar", lambda e, oc=oc, i=i:
                     e.activation(gate_s[:, :], bank(i), AF.Silu,
                                  scale=sg_s[:, oc:oc + 1]),
                     reads=("c:sg_s",), writes=(f"ps{i}", "gate"), tag=tag)
                P.op("vector", lambda e, oc=oc, i=i:
                     e.scalar_tensor_tensor(fuse_v[:, oc, :], bank(2 + i),
                                            su_s[:, oc:oc + 1], gate_s[:, :],
                                            op0=ALU.mult, op1=ALU.mult),
                     reads=("c:su_s", "gate"), writes=(f"ps{2 + i}", f"fu{oc}"),
                     tag=tag)
        for p0 in range(0, OGD, OGP):
            ogs = list(range(p0, min(p0 + OGP, OGD)))
            last_pass = (p0 + OGP >= OGD)
            for ic0 in range(0, ICC, 2):
                if ic_hook is not None:
                    ic_hook(p0, ic0)
                par = (ic0 // 2) % 2
                queue = "sync" if par == 0 else "gpsimd"
                nw = len(ogs) * 512
                dma(queue,
                    wstgD[par][:, 0:2 * nw].rearrange(
                        "p (c d) -> p c d", d=nw),
                    wd_h[ic0 * 128:(ic0 + 2) * 128,
                         p0 * 512:(p0 + len(ogs)) * 512].rearrange(
                        "(c p) d -> p c d", p=128),
                    reads=(), writes=(f"sD{par}",), tag=tag,
                    chan=f"wd{par}")
                P.op("vector" if par == 0 else "scalar",
                     lambda e, par=par, nw=nw:
                     (e.tensor_copy if par == 0 else e.copy)(
                         wdbuf2[par].rearrange("p c d -> p (c d)")[:, 0:2 * nw],
                         wstgD[par][:, 0:2 * nw]),
                     reads=(f"sD{par}",), writes=(f"wA{par}",), tag=tag)
                for sub in range(2):
                    ic = ic0 + sub
                    for gi, og in enumerate(ogs):
                        for ts in range(TS):
                            b = gi * TS + ts
                            mm(fuse_v[:, ic, ts * 128:(ts + 1) * 128],
                               wdbuf2[par][:, sub, gi * 512:(gi + 1) * 512],
                               bank(b), ic == 0, ic == ICC - 1,
                               reads=(f"wA{par}", f"fu{ic}"),
                               writes=(f"ps{b}",), tag=tag)
            for gi, og in enumerate(ogs):
                load_bc("sd", og, tag)
                for ts in range(TS):
                    b = gi * TS + ts
                    P.op("vector", lambda e, b=b:
                         e.tensor_mul(rt1[:, 0:512], bank(b), bcbuf[:, :]),
                         reads=("c:bc",), writes=(f"ps{b}", "rt1"), tag=tag)
                    P.op("vector", lambda e, og=og, ts=ts:
                         e.tensor_add(x_tok[:, ts, og * 512:(og + 1) * 512],
                                      x_tok[:, ts, og * 512:(og + 1) * 512],
                                      rt1[:, 0:512]),
                         reads=("rt1",), writes=(f"x{ts}",), tag=tag)
                    if collect_stats:
                        inc_stats(og, ts, b, tag)
                        if last_pass and gi == len(ogs) - 1 and ts >= 2:
                            ln_finish_ts(ts - 2, tag + "lf")
            if collect_stats and last_pass:
                for ts in (TS - 2, TS - 1):
                    ln_finish_ts(ts, tag + "lf")

    attn_scr_keys = {"Kscr": [], "Qscr": [], "VTscr": [], "x1scr": []}

    def qkv_rope(Ji, tag, mid_hook=None, og_hook=None):
        P.fence(["c:bc"], ["qsb0", "qsb1"])

        def evac_rope(pi, h, i, bname, bvec, dscr, tag):
            qs = q_sb[i]
            P.op("vector", lambda e, h=h, i=i, bvec=bvec, qs=qs:
                 e.tensor_scalar_add(qs[:, :], bank(i), bvec[:, h:h + 1]),
                 reads=(f"c:{bname}",), writes=(f"ps{i}", f"qsb{i}"),
                 tag=tag)

        def rope_tail(pi, h, i, dscr, tag):
            qs = q_sb[i]
            if pi < 2:
                mm(rotm[:], qs[:, :], bank(4 + i)[:, :], True, True,
                   reads=(f"qsb{i}", "c:rot_s"), writes=(f"ps{4 + i}",), tag=tag)
                P.op("vector", lambda e, i=i:
                     e.tensor_mul(rt1[:, :], bank(4 + i)[:, :], sinJ[:, :]),
                     reads=("c:sinJ",), writes=(f"ps{4 + i}", "rt1"), tag=tag)
                P.op("vector", lambda e, qs=qs:
                     e.tensor_mul(gate_s[:, :], qs[:, :], cosJ[:, :]),
                     reads=(f"qsb{i}", "c:cosJ"), writes=("gate",), tag=tag)
                P.op("vector", lambda e, qs=qs:
                     e.tensor_add(qs[:, :], rt1[:, :], gate_s[:, :]),
                     reads=("rt1", "gate"), writes=(f"qsb{i}",), tag=tag)
                lst = attn_scr_keys[dscr.tensor.name]
                uk = f"scrw:{dscr.tensor.name}:{len(lst)}"
                lst.append(uk)
                dma("gpsimd",
                    dscr[h * 128:(h + 1) * 128, Ji * J:(Ji + 1) * J],
                    qs[:, :], reads=(f"qsb{i}",),
                    writes=(uk,), tag=tag,
                    chan=f"scr_{dscr.tensor.name}")
            else:
                for ts in range(TS):
                    transp(bank_bf(6 + i)[:, ts * 128:(ts + 1) * 128],
                           qs[:, ts * 128:(ts + 1) * 128], identb[:],
                           reads=(f"qsb{i}", "c:identb_s"),
                           writes=(f"ps{6 + i}",), tag=tag)
                P.op("vector", lambda e, i=i:
                     e.tensor_copy(vt_sb.rearrange("p a b -> p (a b)"),
                                   bank_bf(6 + i)[:, 0:TS * 128]),
                     reads=(), writes=(f"ps{6 + i}", "gate"), tag=tag)
                lst = attn_scr_keys["VTscr"]
                uk = f"scrw:VTscr:{len(lst)}"
                lst.append(uk)
                dma("gpsimd",
                    VTscr[h, :, Ji * TS:(Ji + 1) * TS, :],
                    vt_sb, reads=("gate",), writes=(uk,),
                    tag=tag, chan="scr_VTscr")

        QROT = [(wbufA[0], "wA0", "sync"), (wbufB[0], "wB0", "gpsimd"),
                (wbufA[1], "wA1", "sync"), (wbufB[1], "wB1", "gpsimd")]
        for pi, (wsrc, bname, bvec, dscr) in enumerate(
                ((qTb, "qb_s", qb_s, Qscr), (kTb, "kb_s", kb_s, Kscr),
                 (vTb, "vb_s", vb_s, None))):
            for og in range(KC // 2):
                if og_hook is not None:
                    og_hook(pi, og)
                buf, bkey, queue = QROT[(pi * (KC // 2) + og) % 4]
                wsl = wsrc[og].rearrange("p (kc o) -> p kc o", o=256)
                dma(queue, buf[:], wsl,
                    reads=(f"scr:{wsrc.tensor.name}:h{og // 4}",),
                    writes=(bkey,),
                    tag=tag, chan=bkey)
                for i in range(2):
                    h = og * 2 + i
                    for kc in range(KC):
                        mm(buf[:, kc, i * 128:(i + 1) * 128],
                           xln_fm[:, kc, :], bank(i), kc == 0, kc == KC - 1,
                           reads=(bkey, f"xlf{kc}"), writes=(f"ps{i}",),
                           tag=tag)
                    evac_rope(pi, h, i, bname, bvec, dscr, tag)
                for i in range(2):
                    rope_tail(pi, og * 2 + i, i, dscr, tag)
            if pi == 0 and mid_hook is not None:
                mid_hook()

    def outproj(tag, collect_stats=False):
        for p0 in range(0, OGD, OGP):
            ogs = list(range(p0, min(p0 + OGP, OGD)))
            for gi, og in enumerate(ogs):
                par = og % 2
                dma("sync", wbufA[par][:],
                    oTb[2 * og].rearrange("p (kc o) -> p kc o", o=256),
                    reads=(f"scr:oTb:h{og // 2}",), writes=(f"wA{par}",),
                    tag=tag, chan=f"wA{par}")
                dma("gpsimd", wbufB[par][:],
                    oTb[2 * og + 1].rearrange("p (kc o) -> p kc o", o=256),
                    reads=(f"scr:oTb:h{(2 * og + 1) // 4}",),
                    writes=(f"wB{par}",), tag=tag,
                    chan=f"wB{par}")
                for kc in range(KC):
                    for ts in range(TS):
                        b = gi * TS + ts
                        mm(O_fm[:, kc, ts * 128:(ts + 1) * 128],
                           wbufA[par][:, kc, :], bank(b)[:, 0:256],
                           kc == 0, False,
                           reads=(f"wA{par}", f"O{kc}"), writes=(f"ps{b}",),
                           tag=tag)
                        mm(O_fm[:, kc, ts * 128:(ts + 1) * 128],
                           wbufB[par][:, kc, :], bank(b)[:, 256:512],
                           False, kc == KC - 1,
                           reads=(f"wB{par}", f"O{kc}"), writes=(f"ps{b}",),
                           tag=tag)
            for gi, og in enumerate(ogs):
                load_bc("ob", og, tag)
                for ts in range(TS):
                    b = gi * TS + ts
                    P.op("vector", lambda e, b=b:
                         e.tensor_add(rt1[:, 0:512], bank(b), bcbuf[:, :]),
                         reads=("c:bc",), writes=(f"ps{b}", "rt1"), tag=tag)
                    P.op("vector", lambda e, og=og, ts=ts:
                         e.tensor_add(x_tok[:, ts, og * 512:(og + 1) * 512],
                                      x_tok[:, ts, og * 512:(og + 1) * 512],
                                      rt1[:, 0:512]),
                         reads=("rt1",), writes=(f"x{ts}",), tag=tag)
                    if collect_stats:
                        inc_stats(og, ts, b, tag)
                        if p0 + OGP >= OGD and gi == len(ogs) - 1 and ts >= 2:
                            ln_finish_ts(ts - 2, tag + "lf")
            if collect_stats and p0 + OGP >= OGD:
                for ts in (TS - 2, TS - 1):
                    ln_finish_ts(ts, tag + "lf")

    def attn_loads(Ji, h, tag):
        par = h % 2
        dma("sync", Kbuf[par][:], Kscr[h * 128:(h + 1) * 128, :],
            reads=("scr:Kscr",), writes=(f"Kbuf{par}",), tag=tag,
            chan=f"Kb{par}")
        dma("sync", Qbuf[par][:],
            Qscr[h * 128:(h + 1) * 128, Ji * J:(Ji + 1) * J],
            reads=("scr:Qscr",), writes=(f"Qbuf{par}",), tag=tag,
            chan=f"Qb{par}")
        dma("sync", Vbuf[par][:], VTscr[h],
            reads=("scr:VTscr",), writes=(f"Vbuf{par}",), tag=tag,
            chan=f"Vb{par}")

    def attn_prelude(h, tag):
        par = h % 2
        nb = (SC + 7) // 8
        for kc in range(SC):
            b = kc // 8
            col = (kc % 8) * 64
            mm(Kbuf[par][:, kc * 128:(kc + 1) * 128], Qbuf[par][:, 0:64],
               bank(b)[:, col:col + 64], True, True,
               reads=(f"Kbuf{par}", f"Qbuf{par}"), writes=(f"ps{b}",), tag=tag)
        for b in range(nb):
            w = min(8, SC - b * 8) * 64
            P.op("vector", lambda e, b=b, w=w:
                 e.reduce_max(m128[:, 1:2], bank(b)[:, 0:w], axis=AX.X),
                 reads=(), writes=(f"ps{b}", "m1"), tag=tag)
            if b == 0:
                P.op("vector", lambda e:
                     e.tensor_copy(m128[:, 0:1], m128[:, 1:2]),
                     reads=("m1",), writes=("m0",), tag=tag)
            else:
                P.op("vector", lambda e:
                     e.tensor_max(m128[:, 0:1], m128[:, 0:1], m128[:, 1:2]),
                     reads=("m1",), writes=("m0",), tag=tag)
        P.op("vector", lambda e:
             e.tensor_copy(m128b[:, 0:1], m128[:, 0:1]),
             reads=("m0",), writes=("m0b",), tag=tag)
        transp(bank_bf(4)[0:1, 0:128], m128b[:, 0:1], identb[:],
               reads=("m0b", "c:identb_s"), writes=("ps4",), tag=tag)
        P.op("vector", lambda e:
             e.reduce_max(crow[0:1, 0:1], bank_bf(4)[0:1, 0:128], axis=AX.X),
             reads=(), writes=("ps4", "cmax"), tag=tag)
        P.op("vector", lambda e:
             e.tensor_scalar_mul(crow[0:1, 1:2], crow[0:1, 0:1], -SCALE),
             reads=("cmax",), writes=("cneg",), tag=tag)
        P.op("vector", lambda e:
             e.tensor_copy(crowb[0:1, 0:1], crow[0:1, 1:2]),
             reads=("cneg",), writes=("cnegb",), tag=tag)
        mm(onesb[0:1, :], crowb[0:1, 0:1], bank(4)[:, 0:1], True, True,
           reads=("cnegb", "c:onesb_s"), writes=("ps4",), tag=tag)
        P.op("vector", lambda e, h=h:
             e.tensor_copy(Chb[:, h:h + 1], bank(4)[:, 0:1]),
             reads=(), writes=("ps4", f"Chb{h}"), tag=tag)

    def attention_phase(Ji, tag):
        prev = None
        for h in range(H + 1):
            if h < H:
                par = h % 2
                attn_loads(Ji, h, tag + f"L{h}")
                if Ji == 0:
                    attn_prelude(h, tag + f"C{h}")
            for kc in range(SC):
                if h < H:
                    if h == 0 and kc == 0:
                        P.fence(KEY_LN + KEY_FU, KEY_PT)
                    pr = kc // 2
                    pp = pr % 2
                    b = pp * 2 + (kc % 2)
                    mm(Kbuf[par][:, kc * 128:(kc + 1) * 128], Qbuf[par][:, :],
                       bank(b), True, True,
                       reads=(f"Kbuf{par}", f"Qbuf{par}"), writes=(f"ps{b}",),
                       tag=tag)
                    if kc % 2 == 1 or kc == SC - 1:
                        n = 1024 if kc % 2 == 1 else 512
                        P.op("scalar",
                             lambda e, pr=pr, pp=pp, h=h, par=par, n=n:
                             e.activation(PT[par][:, pr * 1024:pr * 1024 + n],
                                          pair(pp)[:, 0:n], AF.Exp,
                                          bias=Chb[:, h:h + 1], scale=SCALE),
                             reads=(f"Chb{h}",),
                             writes=(f"ps{pp * 2}", f"ps{pp * 2 + 1}",
                                     f"PT{par}_{pr}"),
                             tag=tag)
                if prev is not None:
                    hp, parp = prev
                    bdn, bpv = 4 + 2 * parp, 5 + 2 * parp
                    prp = kc // 2
                    mm(onesb[:], PT[parp][:, kc * 512:(kc + 1) * 512],
                       bank(bdn), kc == 0, kc == SC - 1,
                       reads=(f"PT{parp}_{prp}", "c:onesb_s"),
                       writes=(f"ps{bdn}",), tag=tag)
                    mm(Vbuf[parp][:, kc, :],
                       PT[parp][:, kc * 512:(kc + 1) * 512], bank(bpv),
                       kc == 0, kc == SC - 1,
                       reads=(f"PT{parp}_{prp}", f"Vbuf{parp}"),
                       writes=(f"ps{bpv}",), tag=tag)
            if prev is not None:
                hp, parp = prev
                bdn, bpv = 4 + 2 * parp, 5 + 2 * parp
                P.op("scalar", lambda e, bdn=bdn:
                     e.activation(rden[:, :], bank(bdn), AF.Ln),
                     reads=(), writes=(f"ps{bdn}", "rt1"), tag=tag)
                P.op("scalar", lambda e:
                     e.activation(rden[:, :], rden[:, :], AF.Exp, scale=-1.0),
                     reads=(), writes=("rt1",), tag=tag)
                P.op("vector", lambda e, hp=hp, bpv=bpv:
                     e.tensor_mul(O_fm[:, hp, :], bank(bpv), rden[:, :]),
                     reads=("rt1",), writes=(f"ps{bpv}", f"O{hp}"), tag=tag)
            prev = (h, h % 2) if h < H else None

    def load_x(Ji, srcten, tag):
        srcap = (x_h if srcten == "x" else x1scr)
        rd = ("scr:x1scr",) if srcten == "x1" else ()
        dma("gpsimd", x_tok[:],
            srcap[Ji * J:(Ji + 1) * J, :].rearrange("(ts p) d -> p ts d", p=128),
            reads=rd, writes=tuple(f"x{t}" for t in range(TS)), tag=tag,
            chan="xtok")

    dma("sync", x_tok[:],
        x_h[0:J, :].rearrange("(ts p) d -> p ts d", p=128),
        reads=(), writes=tuple(f"x{t}" for t in range(TS)), tag="A0x",
        chan="xtok")
    ln_stats("A0ln1")
    ln_apply_tt("A0ln1", early=True)
    ln_apply_tr("A0ln1")
    cv_wgwu_next()

    for Ji in range(NJ):
        tg = f"A{Ji}"
        dma("sync", cosJ[:], cos_h[:, Ji * J:(Ji + 1) * J],
            reads=(), writes=("c:cosJ",), tag=tg, chan="cosJ")
        dma("sync", sinJ[:], sin_h[:, Ji * J:(Ji + 1) * J],
            reads=(), writes=("c:sinJ",), tag=tg, chan="sinJ")
        if Ji > 0:
            ln_apply_tr(tg + "ln1")
        if Ji == 0:
            def _og_hook(og):
                if og % 8 == 2:
                    cv_wgwu_next()
                if og % 8 == 6:
                    cv_qkvo_next()
        else:
            _og_hook = None
        ffn(tg + "f1", collect_stats=True, og_hook=_og_hook)
        ln_apply_tr(tg + "ln2")
        uk = f"scrw:x1scr:{Ji}"
        attn_scr_keys["x1scr"].append(uk)
        dma("gpsimd",
            x1scr[Ji * J:(Ji + 1) * J, :].rearrange("(ts p) d -> p ts d", p=128),
            x_tok[:], reads=tuple(f"x{t}" for t in range(TS)),
            writes=(uk,), tag=tg, chan="scr_x1scr")
        if Ji + 1 < NJ:
            load_x(Ji + 1, "x", tg)

            def _hook(Ji=Ji):
                ln_stats(f"A{Ji + 1}ln1")
                ln_apply_tt(f"A{Ji + 1}ln1")
        else:
            _hook = None
        if Ji == 0:
            def _qhook(pi, og):
                if pi == 0 and og in (0, 2):
                    cv_qkvo_next()
        elif Ji == 1:
            def _qhook(pi, og):
                if pi == 0 and og in (0, 4):
                    cv_qkvo_next()
        else:
            _qhook = None
        qkv_rope(Ji, tg + "qkv", mid_hook=_hook, og_hook=_qhook)

    for name, keys in attn_scr_keys.items():
        P.fence(keys, [f"scr:{name}"])
    P.fence(KEY_P0, ["Kbuf0", "Kbuf1", "Vbuf0", "Vbuf1"])
    for Ji in range(NJ):
        tg = f"B{Ji}"
        if Ji == 0:
            load_x(0, "x1", tg)
        attention_phase(Ji, tg)
        if debug_scratch and Ji == 0:
            dma("gpsimd", dbgO, O_fm.ap().rearrange("p a b -> p (a b)"),
                reads=tuple(f"O{h}" for h in range(H)), writes=("dbgO",),
                tag=tg, chan="dbg")
        outproj(tg + "op", collect_stats=True)
        if debug_scratch:
            dma("gpsimd", dbgX2[Ji * TS:(Ji + 1) * TS].rearrange("a p d -> p a d"),
                x_tok[:], reads=tuple(f"x{t}" for t in range(TS)),
                writes=("dbgX2",), tag=tg, chan="dbg")
        ln_apply_tr(tg + "ln3")
        ffn(tg + "f2")
        dma("gpsimd",
            y_h[Ji * J:(Ji + 1) * J, :].rearrange("(ts p) d -> p ts d", p=128),
            x_tok[:], reads=tuple(f"x{t}" for t in range(TS)),
            writes=(f"y{Ji}",), tag=tg, chan="scr_y")
        if Ji + 1 < NJ:
            load_x(Ji + 1, "x1", tg)

    P.finalize()
    for ch in P.channels:
        sems[ch] = ctx.enter_context(nc.semaphore(f"dch_{ch[1]}_{ch[2]}"))
    with ctx, nc.Block() as blk:
        @blk.sync
        def _(e):
            P.emit_engine("sync", e, sems)

        @blk.gpsimd
        def _(e):
            P.emit_engine("gpsimd", e, sems)

        @blk.tensor
        def _(e):
            P.emit_engine("tensor", e, sems)

        @blk.scalar
        def _(e):
            P.emit_engine("scalar", e, sems)

        @blk.vector
        def _(e):
            P.emit_engine("vector", e, sems)

    return nc, len(P.ops)


def rope_tables(S):
    inv = (np.float32(1.0) /
           (np.float32(10000.0) **
            (np.arange(0, 128, 2, dtype=np.float32) / np.float32(128.0))))
    inv = inv.astype(np.float32)
    t = np.arange(S, dtype=np.float32)
    f = (inv[:, None] * t[None, :]).astype(np.float32)
    cos = np.concatenate([np.cos(f), np.cos(f)], axis=0).astype(ml_dtypes.bfloat16)
    sin = np.concatenate([-np.sin(f), np.sin(f)], axis=0).astype(ml_dtypes.bfloat16)
    return cos, sin


def make_consts(S, D, I, H):
    KC = D // 128
    cos, sin = rope_tables(S)
    rot = np.zeros((128, 128), np.float32)
    rot[0:64, 64:128] = np.eye(64)
    rot[64:128, 0:64] = np.eye(64)
    return {
        "cosT": cos, "sinT": sin,
        "identb": np.eye(128).astype(ml_dtypes.bfloat16),
        "ident32": np.eye(128).astype(np.float32),
        "rot": rot.astype(ml_dtypes.bfloat16),
        "onesb": np.ones((128, 128), ml_dtypes.bfloat16),
    }


def col_layout(v, KC):
    return np.ascontiguousarray(np.asarray(v, np.float32).reshape(KC, 128).T)


def prep_in_maps(inputs, S, D, I, H):
    KC, ICC = D // 128, I // 128
    g = lambda k: np.asarray(inputs[k])
    consts = make_consts(S, D, I, H)
    shared = dict(consts)
    shared["qT"] = np.ascontiguousarray(g("L0_attn_q_w").T.astype(np.float32))
    shared["kT"] = np.ascontiguousarray(g("L0_attn_k_w").T.astype(np.float32))
    shared["vT"] = np.ascontiguousarray(g("L0_attn_v_w").T.astype(np.float32))
    shared["oT"] = np.ascontiguousarray(g("L0_attn_out_w").T.astype(np.float32))
    shared["wg"] = np.ascontiguousarray(g("L0_ffn_w_gate"))
    shared["wu"] = np.ascontiguousarray(g("L0_ffn_w_up"))
    shared["wd"] = np.ascontiguousarray(g("L0_ffn_w_down"))
    shared["qb"] = col_layout(g("L0_attn_q_b"), KC)
    shared["kb"] = col_layout(g("L0_attn_k_b"), KC)
    shared["vb"] = col_layout(g("L0_attn_v_b"), KC)
    shared["ob"] = col_layout(g("L0_attn_out_b"), KC)
    shared["sg"] = col_layout(g("L0_ffn_s_gate"), ICC)
    shared["su"] = col_layout(g("L0_ffn_s_up"), ICC)
    shared["sd"] = col_layout(g("L0_ffn_s_down"), KC)
    shared["sdf"] = np.asarray(g("L0_ffn_s_down"), np.float32).reshape(1, D)
    shared["obf"] = np.asarray(g("L0_attn_out_b"), np.float32).reshape(1, D)
    x = np.asarray(g("L0_attn_x"), np.float32)
    in_maps = []
    for c in range(x.shape[0]):
        m = dict(shared)
        m["x"] = np.ascontiguousarray(x[c])
        in_maps.append(m)
    return in_maps


def kernel(**inputs):
    x = np.asarray(inputs["L0_attn_x"])
    B, S, D = x.shape
    I = np.asarray(inputs["L0_ffn_w_gate"]).shape[1]
    H = D // 128
    nc, n_ops = build_program(S, D, I, H)
    in_maps = prep_in_maps(inputs, S, D, I, H)
    res = run_bass_kernel_spmd(nc, in_maps, core_ids=list(range(B)))
    out = np.stack([np.asarray(r["y"], dtype=np.float32) for r in res.results],
                   axis=0)
    return out


# version 27
# speedup vs baseline: 1.0888x; 1.0888x over previous
import math
from contextlib import ExitStack

import numpy as np
import ml_dtypes

import concourse.bass as bass
import concourse.mybir as mybir
from concourse.bass_utils import run_bass_kernel_spmd

F32 = mybir.dt.float32
BF16 = mybir.dt.bfloat16
I8 = mybir.dt.int8
U8 = mybir.dt.uint8
AF = mybir.ActivationFunctionType
ALU = mybir.AluOpType
AX = mybir.AxisListType

N_CORES = 8
DMA_ENGS = ("sync", "gpsimd")
ALL_ENGS = ("tensor", "scalar", "vector", "gpsimd", "sync")


class _Op:
    __slots__ = ("eng", "fn", "deps", "inc", "idx", "seq", "tag", "rd", "wr",
                 "chan")

    def __init__(self, eng, fn, idx, tag):
        self.eng = eng
        self.fn = fn
        self.deps = []
        self.inc = False
        self.idx = idx
        self.seq = None
        self.tag = tag
        self.chan = None


class Plan:
    def __init__(self):
        self.ops = []
        self.res = {}

    def op(self, eng, fn, reads=(), writes=(), tag="", chan=None):
        o = _Op(eng, fn, len(self.ops), tag)
        if chan is not None:
            assert eng in DMA_ENGS
            o.chan = ("dma", chan, eng)
        dep = {}

        def add(op2):
            if op2 is None:
                return
            if op2.eng == eng and eng not in DMA_ENGS:
                return
            if dep.get(op2.eng) is None or dep[op2.eng].idx < op2.idx:
                dep[op2.eng] = op2

        for k in reads:
            st = self.res.get(k)
            if st:
                for op2 in st["w"].values():
                    add(op2)
        for k in writes:
            st = self.res.get(k)
            if st:
                for op2 in st["w"].values():
                    add(op2)
                for op2 in st["r"].values():
                    add(op2)
        o.deps = list(dep.values())
        o.rd = tuple(reads)
        o.wr = tuple(writes)
        for d in o.deps:
            d.inc = True
        if o.chan is not None:
            o.inc = True
        for k in reads:
            self.res.setdefault(k, {"w": {}, "r": {}})["r"][eng] = o
        for k in writes:
            self.res[k] = {"w": {eng: o}, "r": {}}
        self.ops.append(o)
        return o

    def fence(self, old_keys, new_keys):
        mw, mr = {}, {}
        for k in old_keys:
            st = self.res.get(k)
            if not st:
                continue
            for e, o2 in st["w"].items():
                if e not in mw or mw[e].idx < o2.idx:
                    mw[e] = o2
            for e, o2 in st["r"].items():
                if e not in mr or mr[e].idx < o2.idx:
                    mr[e] = o2
        for k in new_keys:
            st = self.res.setdefault(k, {"w": {}, "r": {}})
            for e, o2 in mw.items():
                if e not in st["w"] or st["w"][e].idx < o2.idx:
                    st["w"][e] = o2
            for e, o2 in mr.items():
                if e not in st["r"] or st["r"][e].idx < o2.idx:
                    st["r"][e] = o2

    def finalize(self):
        cum = {}
        for o in self.ops:
            key = o.chan if o.chan is not None else o.eng
            if o.inc:
                cum[key] = cum.get(key, 0) + (16 if o.chan is not None else 1)
                o.seq = cum[key]
        self.cum = cum
        self.per_eng = {e: [] for e in ALL_ENGS}
        for o in self.ops:
            self.per_eng[o.eng].append(o)
        self.channels = sorted({o.chan for o in self.ops if o.chan is not None})
        self.chan_eng = {}
        for o in self.ops:
            if o.chan is not None:
                self.chan_eng[o.chan] = o.eng

    def emit_engine(self, ename, eng, sems, final_chans=()):
        waited = {}
        for o in self.per_eng[ename]:
            need = {}
            for d in o.deps:
                key = d.chan if d.chan is not None else d.eng
                if need.get(key, 0) < d.seq:
                    need[key] = d.seq
            for s, v in sorted(need.items(), key=lambda kv: str(kv[0])):
                if waited.get(s, 0) < v:
                    eng.wait_ge(sems[s], v)
                    waited[s] = v
            instr = o.fn(eng)
            if o.inc:
                key = o.chan if o.chan is not None else o.eng
                instr.then_inc(sems[key], 16 if o.chan is not None else 1)
        if ename in DMA_ENGS:
            for ch, e2 in sorted(self.chan_eng.items(), key=lambda kv: str(kv[0])):
                if e2 == ename and waited.get(ch, 0) < self.cum[ch]:
                    eng.wait_ge(sems[ch], self.cum[ch])


def build_program(S, D, I, H, debug_scratch=False):
    KC = D // 128
    ICC = I // 128
    SC = S // 128
    J = 512
    TS = J // 128
    NJ = S // J
    OGD = D // 512
    OGP = max(1, min(OGD, 8 // TS))
    SCALE = 1.0 / math.sqrt(128.0)
    EPS = 1e-5
    assert KC == H and D % 512 == 0 and S % J == 0 and I % 256 == 0

    nc = bass.Bass("TRN2", target_bir_lowering=False)

    x_h = nc.dram_tensor("x", [S, D], F32, kind="ExternalInput").ap()
    qT_h = nc.dram_tensor("qT", [D, D], F32, kind="ExternalInput").ap()
    kT_h = nc.dram_tensor("kT", [D, D], F32, kind="ExternalInput").ap()
    vT_h = nc.dram_tensor("vT", [D, D], F32, kind="ExternalInput").ap()
    oT_h = nc.dram_tensor("oT", [D, D], F32, kind="ExternalInput").ap()
    wg_h = nc.dram_tensor("wg", [D, I], I8, kind="ExternalInput").ap()
    wu_h = nc.dram_tensor("wu", [D, I], I8, kind="ExternalInput").ap()
    wd_h = nc.dram_tensor("wd", [I, D], I8, kind="ExternalInput").ap()
    qb_h = nc.dram_tensor("qb", [128, KC], F32, kind="ExternalInput").ap()
    kb_h = nc.dram_tensor("kb", [128, KC], F32, kind="ExternalInput").ap()
    vb_h = nc.dram_tensor("vb", [128, KC], F32, kind="ExternalInput").ap()
    ob_h = nc.dram_tensor("ob", [128, KC], F32, kind="ExternalInput").ap()
    sg_h = nc.dram_tensor("sg", [128, ICC], F32, kind="ExternalInput").ap()
    su_h = nc.dram_tensor("su", [128, ICC], F32, kind="ExternalInput").ap()
    sd_h = nc.dram_tensor("sd", [128, KC], F32, kind="ExternalInput").ap()
    sdf_h = nc.dram_tensor("sdf", [1, D], F32, kind="ExternalInput").ap()
    obf_h = nc.dram_tensor("obf", [1, D], F32, kind="ExternalInput").ap()
    cos_h = nc.dram_tensor("cosT", [128, S], BF16, kind="ExternalInput").ap()
    sin_h = nc.dram_tensor("sinT", [128, S], BF16, kind="ExternalInput").ap()
    idb_h = nc.dram_tensor("identb", [128, 128], BF16, kind="ExternalInput").ap()
    id32_h = nc.dram_tensor("ident32", [128, 128], F32, kind="ExternalInput").ap()
    rot_h = nc.dram_tensor("rot", [128, 128], BF16, kind="ExternalInput").ap()
    onesb_h = nc.dram_tensor("onesb", [128, 128], BF16, kind="ExternalInput").ap()
    y_h = nc.dram_tensor("y", [S, D], F32, kind="ExternalOutput").ap()

    wgb = nc.dram_tensor("wgb", [I // 256, 128, KC * 256], BF16, kind="Internal").ap()
    wub = nc.dram_tensor("wub", [I // 256, 128, KC * 256], BF16, kind="Internal").ap()
    qTb = nc.dram_tensor("qTb", [D // 256, 128, KC * 256], BF16, kind="Internal").ap()
    kTb = nc.dram_tensor("kTb", [D // 256, 128, KC * 256], BF16, kind="Internal").ap()
    vTb = nc.dram_tensor("vTb", [D // 256, 128, KC * 256], BF16, kind="Internal").ap()
    oTb = nc.dram_tensor("oTb", [D // 256, 128, KC * 256], BF16, kind="Internal").ap()
    skind = "ExternalOutput" if debug_scratch else "Internal"
    Kscr = nc.dram_tensor("Kscr", [D, S], BF16, kind=skind).ap()
    Qscr = nc.dram_tensor("Qscr", [D, S], BF16, kind=skind).ap()
    VTscr = nc.dram_tensor("VTscr", [H, 128, SC, 128], BF16, kind=skind).ap()
    x1scr = nc.dram_tensor("x1scr", [S, D], F32, kind=skind).ap()
    dbgO = (nc.dram_tensor("dbgO", [128, H * 512], BF16, kind="ExternalOutput").ap()
            if debug_scratch else None)
    dbgXLN = (nc.dram_tensor("dbgXLN", [128, (D // 128) * 512], BF16,
                             kind="ExternalOutput").ap() if debug_scratch else None)
    dbgST = (nc.dram_tensor("dbgST", [128, 24], F32,
                            kind="ExternalOutput").ap() if debug_scratch else None)
    dbgFU = (nc.dram_tensor("dbgFU", [128, (I // 128) * 512], BF16,
                            kind="ExternalOutput").ap() if debug_scratch else None)
    dbgX2 = (nc.dram_tensor("dbgX2", [S // 128, 128, D], F32,
                            kind="ExternalOutput").ap() if debug_scratch else None)

    ctx = ExitStack()
    sb = lambda n, s, d: ctx.enter_context(nc.sbuf_tensor(n, s, d))

    PT_BYTES = SC * J * 2
    AR_BYTES = max(2 * PT_BYTES, 2 * ICC * J, 8 * D + 2 * TS * D, 40960)
    x_tok = sb("x_tok", [128, TS, D], F32)
    xln_fm = sb("xln_fm", [128, KC, J], BF16)
    arena = sb("arena", [128, AR_BYTES], U8)
    arena_ap = arena.ap()
    sq_v = arena_ap[:, 0:8 * D].bitcast(F32).rearrange("p (a b) -> p a b", b=D)
    xln_tok = arena_ap[:, 8 * D:8 * D + 2 * TS * D].bitcast(BF16).rearrange(
        "p (t d) -> p t d", d=D)
    fuse_v = arena_ap[:, 0:2 * ICC * J].bitcast(BF16).rearrange(
        "p (c j) -> p c j", j=J)
    PT = [arena_ap[:, i * PT_BYTES:(i + 1) * PT_BYTES].bitcast(BF16)
          for i in (0, 1)]
    KEY_LN = ["sq0", "sq1"] + [f"xlt{t}" for t in range(TS)]
    KEY_FU = [f"fu{i}" for i in range(ICC)]
    KEY_PT = [f"PT{p}_{pr}" for p in (0, 1) for pr in range((SC + 1) // 2)]
    KEY_P0 = [f"p0s{s}" for s in range(4)]
    ALL_AR = KEY_LN + KEY_FU + KEY_PT + KEY_P0

    wbufA = [sb(f"wbufA{i}", [128, KC, 256], BF16) for i in (0, 1)]
    wbufB = [sb(f"wbufB{i}", [128, KC, 256], BF16) for i in (0, 1)]
    wdbuf2 = [wbufA[i].ap().rearrange("p a b -> p (a b)")[:, 0:2 * OGP * 512]
              .rearrange("p (c d) -> p c d", d=OGP * 512) for i in (0, 1)]
    gate_s = sb("gate_s", [128, J], F32)
    SHB = max(4 * 6144, 4 * S + 4 * SC * 128)
    shbuf = sb("shbuf", [128, SHB], U8)
    shb_ap = shbuf.ap()
    Kbuf = [shb_ap[:, i * 2 * S:(i + 1) * 2 * S].bitcast(BF16) for i in (0, 1)]
    Vbuf = [shb_ap[:, 4 * S + i * 2 * SC * 128:
                   4 * S + (i + 1) * 2 * SC * 128].bitcast(BF16).rearrange(
        "p (a b) -> p a b", b=128) for i in (0, 1)]
    Qbuf = [sb(f"Qbuf{i}", [128, J], BF16) for i in (0, 1)]
    O_fm = sb("O_fm", [128, H, J], BF16)
    wstgD = [sb(f"wstgD{i}", [128, 2 * OGP * 512], I8).ap() for i in (0, 1)]
    rt1 = sb("rt1", [128, J], F32)
    rden = rt1
    vt_sb = gate_s.ap()[:, 0:TS * 64].bitcast(BF16).rearrange(
        "p (t d) -> p t d", d=128)
    stats = sb("stats", [128, 6 * TS], F32)
    Chb = sb("Chb", [128, H], F32)
    m128 = sb("m128", [128, 2], F32)
    m128b = sb("m128b", [128, 1], BF16)
    crow = sb("crow", [1, 2], F32)
    cosJ = sb("cosJ", [128, J], BF16)
    sinJ = sb("sinJ", [128, J], BF16)
    bcbuf = sb("bcbuf", [128, 512], F32)
    q_sb = [bcbuf.ap()[:, 0:256].bitcast(BF16),
            bcbuf.ap()[:, 256:512].bitcast(BF16)]
    identb = sb("identb_s", [128, 128], BF16)
    rotm = sb("rot_s", [128, 128], BF16)
    onesb = sb("onesb_s", [128, 128], BF16)
    crowb = sb("crowb", [1, 2], BF16)
    qb_s = sb("qb_s", [128, KC], F32)
    kb_s = sb("kb_s", [128, KC], F32)
    vb_s = sb("vb_s", [128, KC], F32)
    sg_s = sb("sg_s", [128, ICC], F32)
    su_s = sb("su_s", [128, ICC], F32)

    pbig = [ctx.enter_context(nc.psum_tensor(f"pp{i}", [128, 1024], F32))
            for i in (0, 1)]
    psin = [ctx.enter_context(nc.psum_tensor(f"pb{i}", [128, 512], F32))
            for i in (4, 5, 6, 7)]

    def bank(i):
        if i < 4:
            return pbig[i // 2].ap()[:, (i % 2) * 512:(i % 2 + 1) * 512]
        return psin[i - 4].ap()

    def bank_bf(i):
        if i < 4:
            return pbig[i // 2].ap().bitcast(BF16)[:, (i % 2) * 1024:(i % 2 + 1) * 1024]
        return psin[i - 4].ap().bitcast(BF16)

    def pair(p):
        return pbig[p].ap()

    sems = {e: ctx.enter_context(nc.semaphore(f"sem_{e}"))
            for e in ("tensor", "scalar", "vector", "gpsimd")}
    P = Plan()

    def dma(queue, out, in_, reads=(), writes=(), tag="", chan=None):
        assert chan is not None
        P.op(queue, lambda e, o=out, i=in_: e.dma_start(out=o, in_=i),
             reads=reads, writes=writes, tag=tag, chan=chan)

    def mm(lhsT, rhs, out, start, stop, reads, writes, tag=""):
        P.op("tensor",
             lambda e, a=lhsT, b=rhs, o=out, s1=start, s2=stop:
             e.matmul(o, a, b, start=s1, stop=s2),
             reads=reads, writes=writes, tag=tag)

    def transp(out, in_, ident, reads, writes, tag=""):
        P.op("tensor",
             lambda e, o=out, i=in_, d=ident: e.transpose(o, i, d),
             reads=reads, writes=writes, tag=tag)

    for dst, src in [(identb, idb_h), (rotm, rot_h),
                     (onesb, onesb_h), (qb_s, qb_h),
                     (kb_s, kb_h), (vb_s, vb_h), (sg_s, sg_h), (su_s, su_h)]:
        dma("sync", dst[:], src, reads=(), writes=(f"c:{dst.name}",),
            tag="p0c", chan=f"c_{dst.name}")

    NQ_WG = (I + 1023) // 1024
    NH_Q = (D + 1023) // 1024
    cv_state = {"sl": [0, 0], "wg": 0, "wd": 0}

    def cv_chunk(src_ap, dst_ap, kc, c0, cw, src_dt, par, key, tag, og_tiled,
                 cast=True):
        sli = cv_state["sl"][par]
        cv_state["sl"][par] ^= 1
        sl = par * 2 + sli
        esz = 4 if src_dt == F32 else 1
        base = sl * 6144
        vin = shb_ap[:, base:base + cw * esz].bitcast(src_dt)
        vout = shb_ap[:, base + cw * esz:base + cw * esz + 2 * cw].bitcast(BF16)
        skey = f"p0s{sl}"
        rows = slice(kc * 128, (kc + 1) * 128)
        dma("sync", vin[:, 0:cw], src_ap[rows, c0:c0 + cw],
            reads=(), writes=(skey,), tag=tag, chan=f"p0in{par}")
        if cast:
            if par == 0:
                P.op("vector", lambda e, o=vout, i=vin, w=cw:
                     e.tensor_copy(o[:, 0:w], i[:, 0:w]),
                     reads=(), writes=(skey,), tag=tag)
            else:
                P.op("scalar", lambda e, o=vout, i=vin, w=cw:
                     e.copy(o[:, 0:w], i[:, 0:w]),
                     reads=(), writes=(skey,), tag=tag)
        vsrc = vout if cast else vin
        if og_tiled:
            out_ap = dst_ap[c0 // 256:(c0 + cw) // 256, :,
                            kc * 256:(kc + 1) * 256].rearrange("og p o -> p og o")
            in_ap = vsrc[:, 0:cw].rearrange("p (og o) -> p og o", o=256)
        else:
            out_ap = dst_ap[rows, c0:c0 + cw]
            in_ap = vsrc[:, 0:cw]
        dma("gpsimd" if par == 0 else "sync", out_ap, in_ap, reads=(skey,),
            writes=(key,), tag=tag, chan=f"scr_{dst_ap.tensor.name}")

    def cv_wgwu(q):
        cw = min(I, 1024)
        for kc in range(KC):
            cv_chunk(wg_h, wgb, kc, q * 1024, cw, I8, 0, f"scr:wgb:q{q}",
                     f"cvg{q}", True)
        for kc in range(KC):
            cv_chunk(wu_h, wub, kc, q * 1024, cw, I8, 1, f"scr:wub:q{q}",
                     f"cvu{q}", True)

    def cv_wgwu_next():
        if cv_state["wg"] < NQ_WG:
            cv_wgwu(cv_state["wg"])
            cv_state["wg"] += 1

    qkvo_groups = [(s, d, hh) for (s, d) in
                   ((qT_h, qTb), (kT_h, kTb), (vT_h, vTb), (oT_h, oTb))
                   for hh in range(NH_Q)]

    def cv_qkvo_next():
        if not qkvo_groups:
            return
        s, d, hh = qkvo_groups.pop(0)
        cw = min(D, 1024)
        for kc in range(KC):
            cv_chunk(s, d, kc, hh * 1024, cw, F32, kc % 2,
                     f"scr:{d.tensor.name}:h{hh}", f"cvq{d.tensor.name}{hh}",
                     True)

    def load_bc(kind, og, tag):
        P.fence(["qsb0", "qsb1"], ["c:bc"])
        srcb = sdf_h if kind == "sd" else obf_h
        dma("sync", bcbuf[:, :],
            srcb[0:1, og * 512:(og + 1) * 512].broadcast_to([128, 512]),
            reads=(), writes=("c:bc",), tag=tag, chan="bcbuf")
        if kind == "sd":
            P.op("scalar", lambda e: e.mul(bcbuf[:, :], bcbuf[:, :], 0.5),
                 reads=(), writes=("c:bc",), tag=tag)

    spart = sb("spart", [128, 2 * TS * OGD], F32)

    def inc_stats(og, ts, b, tag):
        P.op("vector", lambda e, og=og, ts=ts:
             e.reduce_sum(spart[:, ts * OGD + og:ts * OGD + og + 1],
                          x_tok[:, ts, og * 512:(og + 1) * 512], axis=AX.X),
             reads=(f"x{ts}",), writes=("spart_s",), tag=tag)
        P.op("vector", lambda e, og=og, ts=ts, b=b:
             e.scalar_tensor_tensor(
                 bank(b), x_tok[:, ts, og * 512:(og + 1) * 512], 1.0,
                 x_tok[:, ts, og * 512:(og + 1) * 512],
                 op0=ALU.mult, op1=ALU.mult,
                 accum_out=spart[:, TS * OGD + ts * OGD + og:
                                 TS * OGD + ts * OGD + og + 1]),
             reads=(f"x{ts}",), writes=(f"ps{b}", "spart_q"), tag=tag)

    def ln_stats_finish(tag):
        for ts in range(TS):
            P.op("vector", lambda e, ts=ts:
                 e.reduce_sum(stats[:, ts:ts + 1],
                              spart[:, ts * OGD:(ts + 1) * OGD], axis=AX.X),
                 reads=("spart_s",), writes=("st_sum",), tag=tag)
            P.op("vector", lambda e, ts=ts:
                 e.reduce_sum(stats[:, TS + ts:TS + ts + 1],
                              spart[:, TS * OGD + ts * OGD:
                                    TS * OGD + (ts + 1) * OGD], axis=AX.X),
                 reads=("spart_q",), writes=("st_sq",), tag=tag)
        ln_stats_tail(tag)

    def ln_stats(tag):
        P.fence(KEY_FU + KEY_PT, KEY_LN)
        for ts in range(TS):
            sl = ts % 2
            P.op("scalar", lambda e, ts=ts, sl=sl:
                 e.activation(sq_v[:, sl, :], x_tok[:, ts, :], AF.Square),
                 reads=(f"x{ts}",), writes=(f"sq{sl}",), tag=tag)
            P.op("vector", lambda e, ts=ts:
                 e.reduce_sum(stats[:, ts:ts + 1], x_tok[:, ts, :], axis=AX.X),
                 reads=(f"x{ts}",), writes=("st_sum",), tag=tag)
            P.op("vector", lambda e, ts=ts, sl=sl:
                 e.reduce_sum(stats[:, TS + ts:TS + ts + 1], sq_v[:, sl, :],
                              axis=AX.X),
                 reads=(f"sq{sl}",), writes=("st_sq",), tag=tag)
        ln_stats_tail(tag)

    def ln_stats_tail(tag):
        P.op("vector", lambda e:
             e.tensor_scalar_mul(stats[:, 2 * TS:3 * TS], stats[:, 0:TS],
                                 -1.0 / D),
             reads=("st_sum",), writes=("st_nm",), tag=tag)
        P.op("vector", lambda e:
             e.tensor_scalar_mul(stats[:, 4 * TS:5 * TS], stats[:, TS:2 * TS],
                                 1.0 / D),
             reads=("st_sq",), writes=("st_e2",), tag=tag)
        P.op("vector", lambda e:
             e.tensor_mul(stats[:, 5 * TS:6 * TS], stats[:, 2 * TS:3 * TS],
                          stats[:, 2 * TS:3 * TS]),
             reads=("st_nm",), writes=("st_m2",), tag=tag)
        P.op("vector", lambda e:
             e.tensor_sub(stats[:, 4 * TS:5 * TS], stats[:, 4 * TS:5 * TS],
                          stats[:, 5 * TS:6 * TS]),
             reads=("st_m2",), writes=("st_e2",), tag=tag)
        P.op("vector", lambda e:
             e.tensor_scalar_add(stats[:, 4 * TS:5 * TS],
                                 stats[:, 4 * TS:5 * TS], EPS),
             reads=(), writes=("st_e2",), tag=tag)
        P.op("scalar", lambda e:
             e.activation(stats[:, 5 * TS:6 * TS], stats[:, 4 * TS:5 * TS],
                          AF.Ln),
             reads=("st_e2",), writes=("st_m2",), tag=tag)
        P.op("scalar", lambda e:
             e.activation(stats[:, 3 * TS:4 * TS], stats[:, 5 * TS:6 * TS],
                          AF.Exp, scale=-0.5),
             reads=("st_m2",), writes=("st_rs",), tag=tag)

    def ln_apply_tt(tag, early=False):
        P.fence(KEY_FU + KEY_PT, KEY_LN)
        for ts in range(TS):
            eng = "vector" if (ts % 2 == 0 or early) else "gpsimd"
            P.op(eng, lambda e, ts=ts:
                 e.tensor_scalar(xln_tok[:, ts, :], x_tok[:, ts, :],
                                 stats[:, 2 * TS + ts:2 * TS + ts + 1],
                                 stats[:, 3 * TS + ts:3 * TS + ts + 1],
                                 op0=ALU.add, op1=ALU.mult),
                 reads=(f"x{ts}", "st_nm", "st_rs"), writes=(f"xlt{ts}",),
                 tag=tag)

    def ln_apply_tr(tag):
        for cpair in range(KC // 2):
            b = 6 + (cpair % 2)
            for i in range(2):
                c = cpair * 2 + i
                for ts in range(TS):
                    transp(bank_bf(b)[:, (i * TS + ts) * 128:(i * TS + ts + 1) * 128],
                           xln_tok[:, ts, c * 128:(c + 1) * 128], identb[:],
                           reads=(f"xlt{ts}", "c:identb_s"), writes=(f"ps{b}",),
                           tag=tag)
            dst = xln_fm[:, cpair * 2:cpair * 2 + 2, :].rearrange(
                "p a b -> p (a b)")
            if cpair % 2 == 0:
                P.op("scalar", lambda e, d=dst, b=b:
                     e.copy(d, bank_bf(b)[:, 0:1024]),
                     reads=(), writes=(f"ps{b}", f"xlf{cpair * 2}",
                                       f"xlf{cpair * 2 + 1}"), tag=tag)
            else:
                P.op("vector", lambda e, d=dst, b=b:
                     e.tensor_copy(d, bank_bf(b)[:, 0:1024]),
                     reads=(), writes=(f"ps{b}", f"xlf{cpair * 2}",
                                       f"xlf{cpair * 2 + 1}"), tag=tag)

    def ln_apply(tag):
        ln_apply_tt(tag)
        ln_apply_tr(tag)

    def ln_finish_ts(ts, tag):
        P.op("vector", lambda e, ts=ts:
             e.reduce_sum(stats[:, ts:ts + 1],
                          spart[:, ts * OGD:(ts + 1) * OGD], axis=AX.X),
             reads=("spart_s",), writes=("st_sum",), tag=tag)
        P.op("vector", lambda e, ts=ts:
             e.reduce_sum(stats[:, TS + ts:TS + ts + 1],
                          spart[:, TS * OGD + ts * OGD:
                                TS * OGD + (ts + 1) * OGD], axis=AX.X),
             reads=("spart_q",), writes=("st_sq",), tag=tag)
        P.op("vector", lambda e, ts=ts:
             e.tensor_scalar_mul(stats[:, 2 * TS + ts:2 * TS + ts + 1],
                                 stats[:, ts:ts + 1], -1.0 / D),
             reads=("st_sum",), writes=("st_nm",), tag=tag)
        P.op("vector", lambda e, ts=ts:
             e.tensor_scalar_mul(stats[:, 4 * TS + ts:4 * TS + ts + 1],
                                 stats[:, TS + ts:TS + ts + 1], 1.0 / D),
             reads=("st_sq",), writes=("st_e2",), tag=tag)
        P.op("vector", lambda e, ts=ts:
             e.tensor_mul(stats[:, 5 * TS + ts:5 * TS + ts + 1],
                          stats[:, 2 * TS + ts:2 * TS + ts + 1],
                          stats[:, 2 * TS + ts:2 * TS + ts + 1]),
             reads=("st_nm",), writes=("st_m2",), tag=tag)
        P.op("vector", lambda e, ts=ts:
             e.tensor_sub(stats[:, 4 * TS + ts:4 * TS + ts + 1],
                          stats[:, 4 * TS + ts:4 * TS + ts + 1],
                          stats[:, 5 * TS + ts:5 * TS + ts + 1]),
             reads=("st_m2",), writes=("st_e2",), tag=tag)
        P.op("vector", lambda e, ts=ts:
             e.tensor_scalar_add(stats[:, 4 * TS + ts:4 * TS + ts + 1],
                                 stats[:, 4 * TS + ts:4 * TS + ts + 1], EPS),
             reads=(), writes=("st_e2",), tag=tag)
        P.op("scalar", lambda e, ts=ts:
             e.activation(stats[:, 5 * TS + ts:5 * TS + ts + 1],
                          stats[:, 4 * TS + ts:4 * TS + ts + 1], AF.Ln),
             reads=("st_e2",), writes=("st_m2",), tag=tag)
        P.op("scalar", lambda e, ts=ts:
             e.activation(stats[:, 3 * TS + ts:3 * TS + ts + 1],
                          stats[:, 5 * TS + ts:5 * TS + ts + 1], AF.Exp,
                          scale=-0.5),
             reads=("st_m2",), writes=("st_rs",), tag=tag)
        P.fence(KEY_FU + KEY_PT, [f"xlt{ts}"])
        eng = "vector" if ts % 2 == 0 else "gpsimd"
        P.op(eng, lambda e, ts=ts:
             e.tensor_scalar(xln_tok[:, ts, :], x_tok[:, ts, :],
                             stats[:, 2 * TS + ts:2 * TS + ts + 1],
                             stats[:, 3 * TS + ts:3 * TS + ts + 1],
                             op0=ALU.add, op1=ALU.mult),
             reads=(f"x{ts}", "st_nm", "st_rs"), writes=(f"xlt{ts}",),
             tag=tag)

    def ffn(tag, collect_stats=False, og_hook=None, ic_hook=None):
        P.fence(KEY_LN + KEY_PT, KEY_FU)
        for og in range(ICC // 2):
            if og_hook is not None:
                og_hook(og)
            par = og % 2
            gsrc = wgb[og].rearrange("p (kc o) -> p kc o", o=256)
            usrc = wub[og].rearrange("p (kc o) -> p kc o", o=256)
            dma("sync", wbufA[par][:], gsrc, reads=(f"scr:wgb:q{og // 4}",),
                writes=(f"wA{par}",), tag=tag, chan=f"wA{par}")
            dma("gpsimd", wbufB[par][:], usrc, reads=(f"scr:wub:q{og // 4}",),
                writes=(f"wB{par}",), tag=tag, chan=f"wB{par}")
            for i in range(2):
                oc = og * 2 + i
                for kc in range(KC):
                    mm(wbufA[par][:, kc, i * 128:(i + 1) * 128],
                       xln_fm[:, kc, :], bank(i), kc == 0, kc == KC - 1,
                       reads=(f"wA{par}", f"xlf{kc}"), writes=(f"ps{i}",),
                       tag=tag)
                for kc in range(KC):
                    mm(wbufB[par][:, kc, i * 128:(i + 1) * 128],
                       xln_fm[:, kc, :], bank(2 + i), kc == 0, kc == KC - 1,
                       reads=(f"wB{par}", f"xlf{kc}"), writes=(f"ps{2 + i}",),
                       tag=tag)
                P.op("scalar", lambda e, oc=oc, i=i:
                     e.activation(gate_s[:, :], bank(i), AF.Silu,
                                  scale=sg_s[:, oc:oc + 1]),
                     reads=("c:sg_s",), writes=(f"ps{i}", "gate"), tag=tag)
                P.op("vector", lambda e, oc=oc, i=i:
                     e.scalar_tensor_tensor(fuse_v[:, oc, :], bank(2 + i),
                                            su_s[:, oc:oc + 1], gate_s[:, :],
                                            op0=ALU.mult, op1=ALU.mult),
                     reads=("c:su_s", "gate"), writes=(f"ps{2 + i}", f"fu{oc}"),
                     tag=tag)
        for p0 in range(0, OGD, OGP):
            ogs = list(range(p0, min(p0 + OGP, OGD)))
            last_pass = (p0 + OGP >= OGD)
            for ic0 in range(0, ICC, 2):
                if ic_hook is not None:
                    ic_hook(p0, ic0)
                par = (ic0 // 2) % 2
                queue = "sync" if par == 0 else "gpsimd"
                nw = len(ogs) * 512
                dma(queue,
                    wstgD[par][:, 0:2 * nw].rearrange(
                        "p (c d) -> p c d", d=nw),
                    wd_h[ic0 * 128:(ic0 + 2) * 128,
                         p0 * 512:(p0 + len(ogs)) * 512].rearrange(
                        "(c p) d -> p c d", p=128),
                    reads=(), writes=(f"sD{par}",), tag=tag,
                    chan=f"wd{par}")
                P.op("vector" if par == 0 else "scalar",
                     lambda e, par=par, nw=nw:
                     (e.tensor_copy if par == 0 else e.copy)(
                         wdbuf2[par].rearrange("p c d -> p (c d)")[:, 0:2 * nw],
                         wstgD[par][:, 0:2 * nw]),
                     reads=(f"sD{par}",), writes=(f"wA{par}",), tag=tag)
                for sub in range(2):
                    ic = ic0 + sub
                    for gi, og in enumerate(ogs):
                        for ts in range(TS):
                            b = gi * TS + ts
                            mm(fuse_v[:, ic, ts * 128:(ts + 1) * 128],
                               wdbuf2[par][:, sub, gi * 512:(gi + 1) * 512],
                               bank(b), ic == 0, ic == ICC - 1,
                               reads=(f"wA{par}", f"fu{ic}"),
                               writes=(f"ps{b}",), tag=tag)
            for gi, og in enumerate(ogs):
                load_bc("sd", og, tag)
                for ts in range(TS):
                    b = gi * TS + ts
                    P.op("vector", lambda e, b=b:
                         e.tensor_mul(rt1[:, 0:512], bank(b), bcbuf[:, :]),
                         reads=("c:bc",), writes=(f"ps{b}", "rt1"), tag=tag)
                    P.op("vector", lambda e, og=og, ts=ts:
                         e.tensor_add(x_tok[:, ts, og * 512:(og + 1) * 512],
                                      x_tok[:, ts, og * 512:(og + 1) * 512],
                                      rt1[:, 0:512]),
                         reads=("rt1",), writes=(f"x{ts}",), tag=tag)
                    if collect_stats:
                        inc_stats(og, ts, b, tag)
                        if last_pass and gi == len(ogs) - 1 and ts >= 2:
                            ln_finish_ts(ts - 2, tag + "lf")
            if collect_stats and last_pass:
                for ts in (TS - 2, TS - 1):
                    ln_finish_ts(ts, tag + "lf")

    attn_scr_keys = {"Kscr": [], "Qscr": [], "VTscr": [], "x1scr": []}

    def qkv_rope(Ji, tag, mid_hook=None, og_hook=None):
        P.fence(["c:bc"], ["qsb0", "qsb1"])

        def evac_rope(pi, h, i, bname, bvec, dscr, tag):
            qs = q_sb[i]
            P.op("vector", lambda e, h=h, i=i, bvec=bvec, qs=qs:
                 e.tensor_scalar_add(qs[:, :], bank(i), bvec[:, h:h + 1]),
                 reads=(f"c:{bname}",), writes=(f"ps{i}", f"qsb{i}"),
                 tag=tag)

        def rope_tail(pi, h, i, dscr, tag):
            qs = q_sb[i]
            if pi < 2:
                mm(rotm[:], qs[:, :], bank(4 + i)[:, :], True, True,
                   reads=(f"qsb{i}", "c:rot_s"), writes=(f"ps{4 + i}",), tag=tag)
                P.op("vector", lambda e, i=i:
                     e.tensor_mul(rt1[:, :], bank(4 + i)[:, :], sinJ[:, :]),
                     reads=("c:sinJ",), writes=(f"ps{4 + i}", "rt1"), tag=tag)
                P.op("vector", lambda e, qs=qs:
                     e.tensor_mul(gate_s[:, :], qs[:, :], cosJ[:, :]),
                     reads=(f"qsb{i}", "c:cosJ"), writes=("gate",), tag=tag)
                P.op("vector", lambda e, qs=qs:
                     e.tensor_add(qs[:, :], rt1[:, :], gate_s[:, :]),
                     reads=("rt1", "gate"), writes=(f"qsb{i}",), tag=tag)
                lst = attn_scr_keys[dscr.tensor.name]
                uk = f"scrw:{dscr.tensor.name}:{len(lst)}"
                lst.append(uk)
                dma("gpsimd",
                    dscr[h * 128:(h + 1) * 128, Ji * J:(Ji + 1) * J],
                    qs[:, :], reads=(f"qsb{i}",),
                    writes=(uk,), tag=tag,
                    chan=f"scr_{dscr.tensor.name}")
            else:
                for ts in range(TS):
                    transp(bank_bf(6 + i)[:, ts * 128:(ts + 1) * 128],
                           qs[:, ts * 128:(ts + 1) * 128], identb[:],
                           reads=(f"qsb{i}", "c:identb_s"),
                           writes=(f"ps{6 + i}",), tag=tag)
                P.op("vector", lambda e, i=i:
                     e.tensor_copy(vt_sb.rearrange("p a b -> p (a b)"),
                                   bank_bf(6 + i)[:, 0:TS * 128]),
                     reads=(), writes=(f"ps{6 + i}", "gate"), tag=tag)
                lst = attn_scr_keys["VTscr"]
                uk = f"scrw:VTscr:{len(lst)}"
                lst.append(uk)
                dma("gpsimd",
                    VTscr[h, :, Ji * TS:(Ji + 1) * TS, :],
                    vt_sb, reads=("gate",), writes=(uk,),
                    tag=tag, chan="scr_VTscr")

        QROT = [(wbufA[0], "wA0", "sync"), (wbufB[0], "wB0", "gpsimd"),
                (wbufA[1], "wA1", "sync"), (wbufB[1], "wB1", "gpsimd")]
        for pi, (wsrc, bname, bvec, dscr) in enumerate(
                ((qTb, "qb_s", qb_s, Qscr), (kTb, "kb_s", kb_s, Kscr),
                 (vTb, "vb_s", vb_s, None))):
            for og in range(KC // 2):
                if og_hook is not None:
                    og_hook(pi, og)
                buf, bkey, queue = QROT[(pi * (KC // 2) + og) % 4]
                wsl = wsrc[og].rearrange("p (kc o) -> p kc o", o=256)
                dma(queue, buf[:], wsl,
                    reads=(f"scr:{wsrc.tensor.name}:h{og // 4}",),
                    writes=(bkey,),
                    tag=tag, chan=bkey)
                for i in range(2):
                    h = og * 2 + i
                    for kc in range(KC):
                        mm(buf[:, kc, i * 128:(i + 1) * 128],
                           xln_fm[:, kc, :], bank(i), kc == 0, kc == KC - 1,
                           reads=(bkey, f"xlf{kc}"), writes=(f"ps{i}",),
                           tag=tag)
                    evac_rope(pi, h, i, bname, bvec, dscr, tag)
                for i in range(2):
                    rope_tail(pi, og * 2 + i, i, dscr, tag)
            if pi == 0 and mid_hook is not None:
                mid_hook()

    def outproj(tag, collect_stats=False):
        for p0 in range(0, OGD, OGP):
            ogs = list(range(p0, min(p0 + OGP, OGD)))
            for gi, og in enumerate(ogs):
                par = og % 2
                dma("sync", wbufA[par][:],
                    oTb[2 * og].rearrange("p (kc o) -> p kc o", o=256),
                    reads=(f"scr:oTb:h{og // 2}",), writes=(f"wA{par}",),
                    tag=tag, chan=f"wA{par}")
                dma("gpsimd", wbufB[par][:],
                    oTb[2 * og + 1].rearrange("p (kc o) -> p kc o", o=256),
                    reads=(f"scr:oTb:h{(2 * og + 1) // 4}",),
                    writes=(f"wB{par}",), tag=tag,
                    chan=f"wB{par}")
                for kc in range(KC):
                    for ts in range(TS):
                        b = gi * TS + ts
                        mm(O_fm[:, kc, ts * 128:(ts + 1) * 128],
                           wbufA[par][:, kc, :], bank(b)[:, 0:256],
                           kc == 0, False,
                           reads=(f"wA{par}", f"O{kc}"), writes=(f"ps{b}",),
                           tag=tag)
                        mm(O_fm[:, kc, ts * 128:(ts + 1) * 128],
                           wbufB[par][:, kc, :], bank(b)[:, 256:512],
                           False, kc == KC - 1,
                           reads=(f"wB{par}", f"O{kc}"), writes=(f"ps{b}",),
                           tag=tag)
            for gi, og in enumerate(ogs):
                load_bc("ob", og, tag)
                for ts in range(TS):
                    b = gi * TS + ts
                    P.op("vector", lambda e, b=b:
                         e.tensor_add(rt1[:, 0:512], bank(b), bcbuf[:, :]),
                         reads=("c:bc",), writes=(f"ps{b}", "rt1"), tag=tag)
                    P.op("vector", lambda e, og=og, ts=ts:
                         e.tensor_add(x_tok[:, ts, og * 512:(og + 1) * 512],
                                      x_tok[:, ts, og * 512:(og + 1) * 512],
                                      rt1[:, 0:512]),
                         reads=("rt1",), writes=(f"x{ts}",), tag=tag)
                    if collect_stats:
                        inc_stats(og, ts, b, tag)
                        if p0 + OGP >= OGD and gi == len(ogs) - 1 and ts >= 2:
                            ln_finish_ts(ts - 2, tag + "lf")
            if collect_stats and p0 + OGP >= OGD:
                for ts in (TS - 2, TS - 1):
                    ln_finish_ts(ts, tag + "lf")

    def attn_loads(Ji, h, tag):
        par = h % 2
        dma("sync", Kbuf[par][:], Kscr[h * 128:(h + 1) * 128, :],
            reads=("scr:Kscr",), writes=(f"Kbuf{par}",), tag=tag,
            chan=f"Kb{par}")
        dma("sync", Qbuf[par][:],
            Qscr[h * 128:(h + 1) * 128, Ji * J:(Ji + 1) * J],
            reads=("scr:Qscr",), writes=(f"Qbuf{par}",), tag=tag,
            chan=f"Qb{par}")
        dma("sync", Vbuf[par][:], VTscr[h],
            reads=("scr:VTscr",), writes=(f"Vbuf{par}",), tag=tag,
            chan=f"Vb{par}")

    def attn_prelude(h, tag):
        par = h % 2
        nb = (SC + 7) // 8
        for kc in range(SC):
            b = kc // 8
            col = (kc % 8) * 64
            mm(Kbuf[par][:, kc * 128:(kc + 1) * 128], Qbuf[par][:, 0:64],
               bank(b)[:, col:col + 64], True, True,
               reads=(f"Kbuf{par}", f"Qbuf{par}"), writes=(f"ps{b}",), tag=tag)
        for b in range(nb):
            w = min(8, SC - b * 8) * 64
            P.op("vector", lambda e, b=b, w=w:
                 e.reduce_max(m128[:, 1:2], bank(b)[:, 0:w], axis=AX.X),
                 reads=(), writes=(f"ps{b}", "m1"), tag=tag)
            if b == 0:
                P.op("vector", lambda e:
                     e.tensor_copy(m128[:, 0:1], m128[:, 1:2]),
                     reads=("m1",), writes=("m0",), tag=tag)
            else:
                P.op("vector", lambda e:
                     e.tensor_max(m128[:, 0:1], m128[:, 0:1], m128[:, 1:2]),
                     reads=("m1",), writes=("m0",), tag=tag)
        P.op("vector", lambda e:
             e.tensor_copy(m128b[:, 0:1], m128[:, 0:1]),
             reads=("m0",), writes=("m0b",), tag=tag)
        transp(bank_bf(4)[0:1, 0:128], m128b[:, 0:1], identb[:],
               reads=("m0b", "c:identb_s"), writes=("ps4",), tag=tag)
        P.op("vector", lambda e:
             e.reduce_max(crow[0:1, 0:1], bank_bf(4)[0:1, 0:128], axis=AX.X),
             reads=(), writes=("ps4", "cmax"), tag=tag)
        P.op("vector", lambda e:
             e.tensor_scalar_mul(crow[0:1, 1:2], crow[0:1, 0:1], -SCALE),
             reads=("cmax",), writes=("cneg",), tag=tag)
        P.op("vector", lambda e:
             e.tensor_copy(crowb[0:1, 0:1], crow[0:1, 1:2]),
             reads=("cneg",), writes=("cnegb",), tag=tag)
        mm(onesb[0:1, :], crowb[0:1, 0:1], bank(4)[:, 0:1], True, True,
           reads=("cnegb", "c:onesb_s"), writes=("ps4",), tag=tag)
        P.op("vector", lambda e, h=h:
             e.tensor_copy(Chb[:, h:h + 1], bank(4)[:, 0:1]),
             reads=(), writes=("ps4", f"Chb{h}"), tag=tag)

    def attention_phase(Ji, tag):
        prev = None
        for h in range(H + 1):
            if h < H:
                par = h % 2
                attn_loads(Ji, h, tag + f"L{h}")
                if Ji == 0:
                    attn_prelude(h, tag + f"C{h}")
            for kc in range(SC):
                if h < H:
                    if h == 0 and kc == 0:
                        P.fence(KEY_LN + KEY_FU, KEY_PT)
                    pr = kc // 2
                    pp = pr % 2
                    b = pp * 2 + (kc % 2)
                    mm(Kbuf[par][:, kc * 128:(kc + 1) * 128], Qbuf[par][:, :],
                       bank(b), True, True,
                       reads=(f"Kbuf{par}", f"Qbuf{par}"), writes=(f"ps{b}",),
                       tag=tag)
                    if kc % 2 == 1 or kc == SC - 1:
                        n = 1024 if kc % 2 == 1 else 512
                        P.op("scalar",
                             lambda e, pr=pr, pp=pp, h=h, par=par, n=n:
                             e.activation(PT[par][:, pr * 1024:pr * 1024 + n],
                                          pair(pp)[:, 0:n], AF.Exp,
                                          bias=Chb[:, h:h + 1], scale=SCALE),
                             reads=(f"Chb{h}",),
                             writes=(f"ps{pp * 2}", f"ps{pp * 2 + 1}",
                                     f"PT{par}_{pr}"),
                             tag=tag)
                if prev is not None:
                    hp, parp = prev
                    bdn, bpv = 4 + 2 * parp, 5 + 2 * parp
                    prp = kc // 2
                    mm(onesb[:], PT[parp][:, kc * 512:(kc + 1) * 512],
                       bank(bdn), kc == 0, kc == SC - 1,
                       reads=(f"PT{parp}_{prp}", "c:onesb_s"),
                       writes=(f"ps{bdn}",), tag=tag)
                    mm(Vbuf[parp][:, kc, :],
                       PT[parp][:, kc * 512:(kc + 1) * 512], bank(bpv),
                       kc == 0, kc == SC - 1,
                       reads=(f"PT{parp}_{prp}", f"Vbuf{parp}"),
                       writes=(f"ps{bpv}",), tag=tag)
            if prev is not None:
                hp, parp = prev
                bdn, bpv = 4 + 2 * parp, 5 + 2 * parp
                P.op("scalar", lambda e, bdn=bdn:
                     e.activation(rden[:, :], bank(bdn), AF.Ln),
                     reads=(), writes=(f"ps{bdn}", "rt1"), tag=tag)
                P.op("scalar", lambda e:
                     e.activation(rden[:, :], rden[:, :], AF.Exp, scale=-1.0),
                     reads=(), writes=("rt1",), tag=tag)
                P.op("vector", lambda e, hp=hp, bpv=bpv:
                     e.tensor_mul(O_fm[:, hp, :], bank(bpv), rden[:, :]),
                     reads=("rt1",), writes=(f"ps{bpv}", f"O{hp}"), tag=tag)
            prev = (h, h % 2) if h < H else None

    def load_x(Ji, srcten, tag):
        srcap = (x_h if srcten == "x" else x1scr)
        rd = ("scr:x1scr",) if srcten == "x1" else ()
        dma("gpsimd", x_tok[:],
            srcap[Ji * J:(Ji + 1) * J, :].rearrange("(ts p) d -> p ts d", p=128),
            reads=rd, writes=tuple(f"x{t}" for t in range(TS)), tag=tag,
            chan="xtok")

    dma("sync", x_tok[:],
        x_h[0:J, :].rearrange("(ts p) d -> p ts d", p=128),
        reads=(), writes=tuple(f"x{t}" for t in range(TS)), tag="A0x",
        chan="xtok")
    ln_stats("A0ln1")
    ln_apply_tt("A0ln1", early=True)
    ln_apply_tr("A0ln1")
    cv_wgwu_next()

    for Ji in range(NJ):
        tg = f"A{Ji}"
        dma("sync", cosJ[:], cos_h[:, Ji * J:(Ji + 1) * J],
            reads=(), writes=("c:cosJ",), tag=tg, chan="cosJ")
        dma("sync", sinJ[:], sin_h[:, Ji * J:(Ji + 1) * J],
            reads=(), writes=("c:sinJ",), tag=tg, chan="sinJ")
        if Ji > 0:
            ln_apply_tr(tg + "ln1")
        if Ji == 0:
            def _og_hook(og):
                if og % 2 == 0 and og <= 12:
                    cv_wgwu_next()
                if og >= 14 and og % 4 == 2:
                    cv_qkvo_next()
                if og >= 28 and og % 2 == 0:
                    cv_qkvo_next()
        else:
            _og_hook = None
        ffn(tg + "f1", collect_stats=True, og_hook=_og_hook)
        ln_apply_tr(tg + "ln2")
        uk = f"scrw:x1scr:{Ji}"
        attn_scr_keys["x1scr"].append(uk)
        dma("gpsimd",
            x1scr[Ji * J:(Ji + 1) * J, :].rearrange("(ts p) d -> p ts d", p=128),
            x_tok[:], reads=tuple(f"x{t}" for t in range(TS)),
            writes=(uk,), tag=tg, chan="scr_x1scr")
        if Ji + 1 < NJ:
            load_x(Ji + 1, "x", tg)

            def _hook(Ji=Ji):
                ln_stats(f"A{Ji + 1}ln1")
                ln_apply_tt(f"A{Ji + 1}ln1")
        else:
            _hook = None
        if Ji == 1:
            def _qhook(pi, og):
                if pi == 0 and og in (0, 4):
                    cv_qkvo_next()
        else:
            _qhook = None
        qkv_rope(Ji, tg + "qkv", mid_hook=_hook, og_hook=_qhook)

    for name, keys in attn_scr_keys.items():
        P.fence(keys, [f"scr:{name}"])
    P.fence(KEY_P0, ["Kbuf0", "Kbuf1", "Vbuf0", "Vbuf1"])
    for Ji in range(NJ):
        tg = f"B{Ji}"
        if Ji == 0:
            load_x(0, "x1", tg)
        attention_phase(Ji, tg)
        if debug_scratch and Ji == 0:
            dma("gpsimd", dbgO, O_fm.ap().rearrange("p a b -> p (a b)"),
                reads=tuple(f"O{h}" for h in range(H)), writes=("dbgO",),
                tag=tg, chan="dbg")
        outproj(tg + "op", collect_stats=True)
        if debug_scratch:
            dma("gpsimd", dbgX2[Ji * TS:(Ji + 1) * TS].rearrange("a p d -> p a d"),
                x_tok[:], reads=tuple(f"x{t}" for t in range(TS)),
                writes=("dbgX2",), tag=tg, chan="dbg")
        ln_apply_tr(tg + "ln3")
        ffn(tg + "f2")
        dma("gpsimd",
            y_h[Ji * J:(Ji + 1) * J, :].rearrange("(ts p) d -> p ts d", p=128),
            x_tok[:], reads=tuple(f"x{t}" for t in range(TS)),
            writes=(f"y{Ji}",), tag=tg, chan="scr_y")
        if Ji + 1 < NJ:
            load_x(Ji + 1, "x1", tg)

    P.finalize()
    for ch in P.channels:
        sems[ch] = ctx.enter_context(nc.semaphore(f"dch_{ch[1]}_{ch[2]}"))
    with ctx, nc.Block() as blk:
        @blk.sync
        def _(e):
            P.emit_engine("sync", e, sems)

        @blk.gpsimd
        def _(e):
            P.emit_engine("gpsimd", e, sems)

        @blk.tensor
        def _(e):
            P.emit_engine("tensor", e, sems)

        @blk.scalar
        def _(e):
            P.emit_engine("scalar", e, sems)

        @blk.vector
        def _(e):
            P.emit_engine("vector", e, sems)

    return nc, len(P.ops)


def rope_tables(S):
    inv = (np.float32(1.0) /
           (np.float32(10000.0) **
            (np.arange(0, 128, 2, dtype=np.float32) / np.float32(128.0))))
    inv = inv.astype(np.float32)
    t = np.arange(S, dtype=np.float32)
    f = (inv[:, None] * t[None, :]).astype(np.float32)
    cos = np.concatenate([np.cos(f), np.cos(f)], axis=0).astype(ml_dtypes.bfloat16)
    sin = np.concatenate([-np.sin(f), np.sin(f)], axis=0).astype(ml_dtypes.bfloat16)
    return cos, sin


def make_consts(S, D, I, H):
    KC = D // 128
    cos, sin = rope_tables(S)
    rot = np.zeros((128, 128), np.float32)
    rot[0:64, 64:128] = np.eye(64)
    rot[64:128, 0:64] = np.eye(64)
    return {
        "cosT": cos, "sinT": sin,
        "identb": np.eye(128).astype(ml_dtypes.bfloat16),
        "ident32": np.eye(128).astype(np.float32),
        "rot": rot.astype(ml_dtypes.bfloat16),
        "onesb": np.ones((128, 128), ml_dtypes.bfloat16),
    }


def col_layout(v, KC):
    return np.ascontiguousarray(np.asarray(v, np.float32).reshape(KC, 128).T)


def prep_in_maps(inputs, S, D, I, H):
    KC, ICC = D // 128, I // 128
    g = lambda k: np.asarray(inputs[k])
    consts = make_consts(S, D, I, H)
    shared = dict(consts)
    shared["qT"] = np.ascontiguousarray(g("L0_attn_q_w").T.astype(np.float32))
    shared["kT"] = np.ascontiguousarray(g("L0_attn_k_w").T.astype(np.float32))
    shared["vT"] = np.ascontiguousarray(g("L0_attn_v_w").T.astype(np.float32))
    shared["oT"] = np.ascontiguousarray(g("L0_attn_out_w").T.astype(np.float32))
    shared["wg"] = np.ascontiguousarray(g("L0_ffn_w_gate"))
    shared["wu"] = np.ascontiguousarray(g("L0_ffn_w_up"))
    shared["wd"] = np.ascontiguousarray(g("L0_ffn_w_down"))
    shared["qb"] = col_layout(g("L0_attn_q_b"), KC)
    shared["kb"] = col_layout(g("L0_attn_k_b"), KC)
    shared["vb"] = col_layout(g("L0_attn_v_b"), KC)
    shared["ob"] = col_layout(g("L0_attn_out_b"), KC)
    shared["sg"] = col_layout(g("L0_ffn_s_gate"), ICC)
    shared["su"] = col_layout(g("L0_ffn_s_up"), ICC)
    shared["sd"] = col_layout(g("L0_ffn_s_down"), KC)
    shared["sdf"] = np.asarray(g("L0_ffn_s_down"), np.float32).reshape(1, D)
    shared["obf"] = np.asarray(g("L0_attn_out_b"), np.float32).reshape(1, D)
    x = np.asarray(g("L0_attn_x"), np.float32)
    in_maps = []
    for c in range(x.shape[0]):
        m = dict(shared)
        m["x"] = np.ascontiguousarray(x[c])
        in_maps.append(m)
    return in_maps


def kernel(**inputs):
    x = np.asarray(inputs["L0_attn_x"])
    B, S, D = x.shape
    I = np.asarray(inputs["L0_ffn_w_gate"]).shape[1]
    H = D // 128
    nc, n_ops = build_program(S, D, I, H)
    in_maps = prep_in_maps(inputs, S, D, I, H)
    res = run_bass_kernel_spmd(nc, in_maps, core_ids=list(range(B)))
    out = np.stack([np.asarray(r["y"], dtype=np.float32) for r in res.results],
                   axis=0)
    return out
